# version 01
# baseline (speedup 1.0000x reference)
import contextlib
import numpy as np
import concourse.bass as bass
import concourse.mybir as mybir
from concourse.bass_utils import run_bass_kernel_spmd

F32 = mybir.dt.float32
BF16 = mybir.dt.bfloat16
AF = mybir.ActivationFunctionType
ALU = mybir.AluOpType
AX = mybir.AxisListType

T = 2048
D = 1024
NCH = 8
NTT = 16
NH = 16
EH = 2112
DFF = 4096
NST = 16
SHIFT = -8.0
RT = bass.ts


class _Op:
    __slots__ = ("eng", "kind", "fn", "deps", "need", "seq")

    def __init__(self, eng, kind, fn, deps):
        self.eng, self.kind, self.fn = eng, kind, fn
        self.deps = [d for d in deps if d is not None]
        self.need = False
        self.seq = None


class KB:
    ENGS = ("sync", "gpsimd", "vector", "scalar", "tensor")

    def __init__(self, name="prog"):
        self.nc = bass.Bass("TRN2", target_bir_lowering=False)
        self.es = contextlib.ExitStack()
        self.ops = {e: [] for e in self.ENGS}
        self.name = name

    def sb(self, scope, name, shape, dtype):
        return scope.enter_context(self.nc.sbuf_tensor(name, shape, dtype))

    def ps(self, scope, name, shape, dtype=F32):
        return scope.enter_context(self.nc.psum_tensor(name, shape, dtype))

    def dram(self, name, shape, dtype, kind):
        return self.nc.dram_tensor(name, shape, dtype, kind=kind).ap()

    def kcol(self, value):
        if not hasattr(self, "_kc"):
            self._kc = {}
            self._kc_sc = contextlib.ExitStack()
            self.es.enter_context(self._kc_sc)
        if value not in self._kc:
            t = self.sb(self._kc_sc, f"kc_{len(self._kc)}", [128, 1], F32)
            op = self.emit("gpsimd", lambda e, t=t, value=value: e.memset(t[:], value))
            self._kc[value] = (t[:], op)
        return self._kc[value]

    def emit(self, eng, fn, deps=(), dma=False):
        op = _Op(eng, "d" if dma else "c", fn, deps)
        self.ops[eng].append(op)
        return op

    def dma(self, eng, out, in_, deps=(), **kw):
        return self.emit(eng, lambda e: e.dma_start(out=out, in_=in_, **kw),
                         deps, dma=True)

    @staticmethod
    def _implied(d, op):
        if d.eng != op.eng:
            return False
        if d.kind == "d" and op.kind == "d":
            return True
        return d.kind == "c" and op.kind == "c" and op.eng == "tensor"

    def finalize(self):
        for eng in self.ENGS:
            for op in self.ops[eng]:
                for d in op.deps:
                    if not self._implied(d, op):
                        d.need = True
        sems = {}
        for eng in self.ENGS:
            cc = cd = 0
            for op in self.ops[eng]:
                if op.kind == "d":
                    cd += 1
                    op.seq = cd * 16
                    op.need = True
                elif op.need:
                    cc += 1
                    op.seq = cc
            if cd:
                sems[(eng, "d")] = self.es.enter_context(
                    self.nc.semaphore(f"s_{eng}_d"))
            if cc:
                sems[(eng, "c")] = self.es.enter_context(
                    self.nc.semaphore(f"s_{eng}_c"))
        blk = self.es.enter_context(self.nc.Block())

        def emit_engine(e, eng):
            for op in self.ops[eng]:
                waits = {}
                for d in op.deps:
                    if self._implied(d, op):
                        continue
                    key = (d.eng, d.kind)
                    waits[key] = max(waits.get(key, 0), d.seq)
                for key, val in sorted(waits.items()):
                    e.wait_ge(sems[key], val)
                instr = op.fn(e)
                if op.need:
                    instr.then_inc(sems[(eng, op.kind)], 16 if op.kind == "d" else 1)

        for eng in self.ENGS:
            if self.ops[eng]:
                getattr(blk, eng)(lambda e, _eng=eng: emit_engine(e, _eng))

    def run(self, in_maps, core_ids=None, **kw):
        if core_ids is None:
            core_ids = list(range(len(in_maps)))
        return run_bass_kernel_spmd(self.nc, in_maps, core_ids=core_ids, **kw)



def emit_ln_tm(k, sc, x_hbm, xn_scr, pre=(), war=()):
    x_t = k.sb(sc, "ln_x", [128, 2 * D], F32)
    xn_t = k.sb(sc, "ln_xn", [128, 2 * D], BF16)
    sq_t = k.sb(sc, "ln_sq", [128, D], F32)
    cols = k.sb(sc, "ln_cols", [128, 16], F32)
    writes = []
    prev = {"cast": [None, None], "r2": None, "ln": None, "rstd_rd": [None, None],
            "wr": [None, None]}
    for i in range(NTT):
        b = i % 2
        co = (i % 2) * 8
        xt = x_t[:, b * D:(b + 1) * D]
        xnt = xn_t[:, b * D:(b + 1) * D]
        ldx = k.dma("sync", xt, x_hbm[RT(i, 128)],
                    deps=list(pre) + list(war) + [prev["cast"][b]])
        sq = k.emit("scalar", lambda e, xt=xt: e.activation(sq_t[:], xt, AF.Square),
                    deps=[ldx, prev["r2"]])
        r1 = k.emit("vector", lambda e, xt=xt, co=co: e.tensor_reduce(
            cols[:, co:co + 1], xt, AX.X, ALU.add), deps=[ldx])
        r2 = k.emit("vector", lambda e, co=co: e.tensor_reduce(
            cols[:, co + 1:co + 2], sq_t[:], AX.X, ALU.add), deps=[sq])
        prev["r2"] = r2
        m = k.emit("vector", lambda e, co=co: e.tensor_scalar(
            cols[:, co:co + 1], cols[:, co:co + 1], 1.0 / D, None, ALU.mult), deps=[r1])
        mm = k.emit("vector", lambda e, co=co: e.tensor_tensor(
            cols[:, co + 3:co + 4], cols[:, co:co + 1], cols[:, co:co + 1], ALU.mult),
            deps=[m])
        var = k.emit("vector", lambda e, co=co: e.tensor_scalar(
            cols[:, co + 2:co + 3], cols[:, co + 1:co + 2], 1.0 / D, None, ALU.mult),
            deps=[r2, prev["ln"]])
        var2 = k.emit("vector", lambda e, co=co: e.tensor_tensor(
            cols[:, co + 2:co + 3], cols[:, co + 2:co + 3], cols[:, co + 3:co + 4],
            ALU.subtract), deps=[mm])
        eps_ap, eps_op = k.kcol(1e-5)
        ln = k.emit("scalar", lambda e, co=co: e.activation(
            cols[:, co + 4:co + 5], cols[:, co + 2:co + 3], AF.Ln, bias=eps_ap),
            deps=[var2, eps_op])
        prev["ln"] = ln
        rstd = k.emit("scalar", lambda e, co=co: e.activation(
            cols[:, co + 5:co + 6], cols[:, co + 4:co + 5], AF.Exp, scale=-0.5),
            deps=[ln, prev["rstd_rd"][b]])
        cast = k.emit("vector", lambda e, xt=xt, xnt=xnt, co=co: e.tensor_scalar(
            xnt, xt, cols[:, co:co + 1], cols[:, co + 5:co + 6],
            ALU.subtract, ALU.mult), deps=[rstd, prev["wr"][b]])
        prev["rstd_rd"][b] = cast
        prev["cast"][b] = cast
        wr = k.dma("gpsimd", xn_scr[RT(i, 128)], xnt, deps=[cast])
        prev["wr"][b] = wr
        writes.append(wr)
    return writes, prev["cast"]


def build_stage_a(debug=False):
    k = KB("stage_a")
    x_hbm = k.dram("x", [T, D], F32, "ExternalInput")
    wkqv = k.dram("Wkqv", [NH, D, EH], F32, "ExternalInput")
    bkqv = k.dram("bkqv", [NH, EH], F32, "ExternalInput")
    wp_h = k.dram("Wp", [D, D], F32, "ExternalInput")
    bp_h = k.dram("bp", [D], F32, "ExternalInput")
    xn_out = k.dram("xn_tm", [T, D], BF16, "ExternalOutput")
    s0_out = k.dram("s0_tm", [T, D], F32, "ExternalOutput")
    if debug:
        sa_dbg = k.dram("sa_dbg", [128, NCH * T], BF16, "ExternalOutput")
        v_dbg = k.dram("v_dbg", [128, NTT * D], BF16, "ExternalOutput")

    with k.es:
        P = k.es
        k.kcol(1e-5)
        k.kcol(SHIFT)
        xn_fm = k.sb(P, "xn_fm", [128, NCH * T], BF16)
        v_tm = k.sb(P, "v_tm", [128, NTT * D], BF16)
        sa_fm = k.sb(P, "sa_fm", [128, NCH * T], BF16)
        ones_row = k.sb(P, "ones_row", [1, 512], BF16)
        ones_rf = k.sb(P, "ones_rf", [1, 64], F32)
        ones_col = k.sb(P, "ones_col", [128, 1], BF16)
        masks = k.sb(P, "masks", [128, 4 * 512], BF16)
        msk_sc = contextlib.ExitStack()
        iot_f = k.sb(msk_sc, "iot_f", [128, 512], F32)
        iot_p = k.sb(msk_sc, "iot_p", [128, 1], F32)
        diff_t = k.sb(msk_sc, "diff_t", [128, 512], F32)

        c_if = k.emit("gpsimd", lambda e: e.iota(
            iot_f[:], [[1, 512]], channel_multiplier=0,
            allow_small_or_imprecise_dtypes=True))
        c_ip = k.emit("gpsimd", lambda e: e.iota(
            iot_p[:], [[1, 1]], channel_multiplier=1,
            allow_small_or_imprecise_dtypes=True))
        k.emit("gpsimd", lambda e: e.memset(ones_row[:], 1.0))
        c_orf = k.emit("gpsimd", lambda e: e.memset(ones_rf[:], 1.0))
        c_oc = k.emit("gpsimd", lambda e: e.memset(ones_col[:], 1.0))
        c_d = k.emit("vector", lambda e: e.tensor_scalar(
            diff_t[:], iot_f[:], iot_p[:], None, ALU.subtract), deps=[c_if, c_ip])
        mask_last = None
        for d in range(4):
            mask_last = k.emit("vector", lambda e, d=d: e.tensor_scalar(
                masks[:, d * 512:(d + 1) * 512], diff_t[:], float(128 * d),
                None, ALU.is_ge), deps=[c_d])

        msk_sc.close()
        ln_sc = contextlib.ExitStack()
        xn_writes, ln_last = emit_ln_tm(k, ln_sc, x_hbm, xn_out, war=[mask_last, c_d])
        ln_sc.close()

        xn_loads = []
        for c in range(NCH):
            xn_loads.append(k.dma("sync", xn_fm[:, c * T:(c + 1) * T],
                                  xn_out[:, RT(c, 128)],
                                  deps=xn_writes if c == 0 else [], transpose=True))

        wv_sc = contextlib.ExitStack()
        wv_f = k.sb(wv_sc, "wv_f", [128, NCH * 64], F32)
        wv_b = k.sb(wv_sc, "wv_b", [128, NCH * NH * 64], BF16)
        bv_f = k.sb(wv_sc, "bv_f", [1, NH * 64], F32)
        bv_b = k.sb(wv_sc, "bv_b", [1, NH * 64], BF16)
        ps_sc_ = contextlib.ExitStack()
        ps_a = k.ps(ps_sc_, "ps_a", [128, 2 * 512])
        ps_b2 = k.ps(ps_sc_, "ps_b2", [128, 2 * 512])
        ldbv = k.dma("gpsimd", bv_f[:].rearrange("o (h e) -> o h e", h=NH),
                     bkqv[:, 2048:2112].unsqueeze(0))
        cbv = k.emit("vector", lambda e: e.tensor_copy(bv_b[:], bv_f[:]), deps=[ldbv])
        wv_casts = []
        prev_ld = None
        for h in range(NH):
            src = wkqv[h].rearrange("(c p) e -> p c e", p=128)[:, :, 2048:2112]
            ld = k.dma("sync", wv_f[:].rearrange("p (c f) -> p c f", c=NCH), src,
                       deps=(list(ln_last) if h == 0 else [prev_ld]))
            ca = k.emit("vector", lambda e, h=h: e.tensor_copy(
                wv_b[:].rearrange("p (c g f) -> p c g f", c=NCH, g=NH)[:, :, h],
                wv_f[:].rearrange("p (c f) -> p c f", c=NCH)), deps=[ld])
            prev_ld = ca
            wv_casts.append(ca)
        v_evs = [None, None]
        for i in range(NTT):
            psv = [ps_a, ps_b2][i % 2]
            mm = None
            for half in range(2):
                for c in range(NCH):
                    deps = []
                    if half == 0 and c == 0:
                        deps = [wv_casts[-1], xn_loads[-1], v_evs[i % 2]]
                    k.emit("tensor",
                           lambda e, i=i, c=c, half=half, psv=psv: e.matmul(
                               psv[:, half * 512:(half + 1) * 512],
                               xn_fm[:, c * T + i * 128: c * T + (i + 1) * 128],
                               wv_b[:, c * NH * 64 + half * 512:
                                    c * NH * 64 + (half + 1) * 512],
                               start=(c == 0), stop=False), deps=deps)
                mm = k.emit("tensor", lambda e, half=half, psv=psv: e.matmul(
                    psv[:, half * 512:(half + 1) * 512],
                    ones_row[0:1, 0:128], bv_b[:, half * 512:(half + 1) * 512],
                    start=False, stop=True), deps=[cbv] if half == 0 else [])
            ev = k.emit("scalar", lambda e, i=i, psv=psv: e.copy(
                v_tm[:, i * D:(i + 1) * D], psv[:, 0:1024]), deps=[mm])
            v_evs[i % 2] = ev

        wv_sc.close()
        at_sc = contextlib.ExitStack()
        k_fm = k.sb(at_sc, "k_fm", [128, NCH * T], BF16)
        q_fm = k.sb(at_sc, "q_fm", [128, NCH * T], BF16)
        wblk_f = k.sb(at_sc, "wblk_f", [128, 2 * 1024], F32)
        wblk_b = k.sb(at_sc, "wblk_b", [128, 2 * 1024], BF16)
        p_sb = k.sb(at_sc, "p_sb", [128, 2 * 512], BF16)
        dn_sb = k.sb(at_sc, "dn_sb", [1, T], F32)
        bc_sb = k.sb(at_sc, "bc_sb", [64, 2 * 512], F32)
        br_f = k.sb(at_sc, "br_f", [1, EH], F32)
        br_b = k.sb(at_sc, "br_b", [1, EH], BF16)
        ps_sa = k.ps(ps_sc_, "ps_sa", [65, T])

        prev = {"wcast": [None, None], "blkmm": [None, None],
                "kqev": [None, None], "exp": [None, None], "pml": [None, None],
                "bcc": [None, None], "nrm": [None, None], "brow": None}
        hd_done = None
        sa_last = None
        exp_any = [None]
        for h in range(NH):
            ldbr = k.dma("gpsimd", br_f[:], bkqv[h].unsqueeze(0),
                         deps=[prev["brow"]] if h else [v_evs[0], v_evs[1]])
            cbr = k.emit("vector", lambda e: e.tensor_copy(br_b[:], br_f[:]),
                         deps=[ldbr])
            kq_ev_last = None
            for bb in range(16):
                ib = (h * 16 + bb) % 2
                src = wkqv[h].rearrange("(c p) e -> p c e", p=128)[:, :, bb * 128:(bb + 1) * 128]
                dstf = wblk_f[:].rearrange("p (i c f) -> p i c f", i=2, c=NCH)[:, ib]
                ldw = k.dma("sync", dstf, src,
                            deps=[prev["wcast"][ib]] if prev["wcast"][ib]
                            else [v_evs[0], v_evs[1]])
                wcast = k.emit("vector", lambda e, ib=ib: e.tensor_copy(
                    wblk_b[:, ib * 1024:(ib + 1) * 1024],
                    wblk_f[:, ib * 1024:(ib + 1) * 1024]),
                    deps=[ldw, prev["blkmm"][ib]])
                prev["wcast"][ib] = wcast
                for w in range(4):
                    iw = (bb * 4 + w) % 2
                    pskq = ps_a[:, iw * 512:(iw + 1) * 512]
                    deps0 = [wcast, v_evs[0], v_evs[1], prev["kqev"][iw],
                             hd_done]
                    mm = None
                    for c in range(NCH):
                        mm = k.emit("tensor",
                                    lambda e, c=c, w=w, pskq=pskq, ib=ib: e.matmul(
                                        pskq,
                                        wblk_b[:, ib * 1024 + c * 128:
                                               ib * 1024 + (c + 1) * 128],
                                        xn_fm[:, c * T + w * 512: c * T + (w + 1) * 512],
                                        start=(c == 0), stop=False),
                                    deps=deps0 if c == 0 else [])
                    mm = k.emit("tensor", lambda e, bb=bb, pskq=pskq: e.matmul(
                        pskq, br_b[:, bb * 128:(bb + 1) * 128], ones_row[:],
                        start=False, stop=True), deps=[cbr])
                    prev["blkmm"][ib] = mm
                    if bb < 8:
                        ev = k.emit("scalar", lambda e, bb=bb, w=w, pskq=pskq: e.copy(
                            k_fm[:, bb * T + w * 512: bb * T + (w + 1) * 512], pskq),
                            deps=[mm, sa_last])
                    else:
                        ev = k.emit("scalar", lambda e, bb=bb, w=w, pskq=pskq: e.activation(
                            q_fm[:, (bb - 8) * T + w * 512: (bb - 8) * T + (w + 1) * 512],
                            pskq, AF.Copy, scale=1.0 / 32.0), deps=[mm, sa_last])
                    prev["kqev"][iw] = ev
                    kq_ev_last = ev
            nsc = 0
            pv_last = None
            dn_last = None
            for w in range(4):
                nkt = 4 * w + 4
                for kt in range(nkt):
                    isc = nsc % 2
                    pssc = ps_b2[:, isc * 512:(isc + 1) * 512]
                    deps0 = [kq_ev_last, prev["exp"][isc], hd_done]
                    mm = None
                    for c in range(NCH):
                        mm = k.emit("tensor",
                                    lambda e, c=c, kt=kt, w=w, pssc=pssc: e.matmul(
                                        pssc,
                                        k_fm[:, c * T + kt * 128: c * T + (kt + 1) * 128],
                                        q_fm[:, c * T + w * 512: c * T + (w + 1) * 512],
                                        start=(c == 0), stop=(c == NCH - 1)),
                                    deps=deps0 if c == 0 else [])
                    pdst = p_sb[:, isc * 512:(isc + 1) * 512]
                    sh_ap, sh_op = k.kcol(SHIFT)
                    ex = k.emit("scalar", lambda e, pssc=pssc, pdst=pdst, sh_ap=sh_ap:
                                e.activation(pdst, pssc, AF.Exp, bias=sh_ap),
                                deps=[mm, prev["pml"][isc], sh_op])
                    prev["exp"][isc] = ex
                    exp_any[0] = ex
                    pm = ex
                    if kt >= 4 * w:
                        dm = kt - 4 * w
                        pm = k.emit("vector", lambda e, pdst=pdst, dm=dm: e.tensor_tensor(
                            pdst, pdst, masks[:, dm * 512:(dm + 1) * 512], ALU.mult),
                            deps=[ex])
                    pv_mm = k.emit("tensor", lambda e, kt=kt, w=w, pdst=pdst, h=h,
                                   nkt=nkt: e.matmul(
                        ps_sa[0:64, w * 512:(w + 1) * 512],
                        v_tm[:, kt * D + h * 64: kt * D + h * 64 + 64], pdst,
                        start=(kt == 0), stop=(kt == nkt - 1)),
                        deps=[pm] + ([hd_done] if kt == 0 else []))
                    dn_mm = k.emit("tensor", lambda e, kt=kt, w=w, pdst=pdst,
                                   nkt=nkt: e.matmul(
                        ps_sa[64:65, w * 512:(w + 1) * 512], ones_col[:], pdst,
                        start=(kt == 0), stop=(kt == nkt - 1), skip_group_check=True),
                        deps=[c_oc] if (h == 0 and nsc == 0) else [])
                    prev["pml"][isc] = dn_mm
                    pv_last = pv_mm
                    dn_last = dn_mm
                    nsc += 1
            dcp = k.emit("scalar", lambda e: e.copy(dn_sb[:], ps_sa[64:65, :]),
                         deps=[dn_last, pv_last, hd_done])
            drc = k.emit("vector", lambda e: e.reciprocal(dn_sb[:], dn_sb[:]),
                         deps=[dcp])
            hd_ops = []
            for w in range(4):
                hw = w % 2
                psbc = ps_b2[0:64, hw * 512:(hw + 1) * 512]
                bcm = k.emit("tensor", lambda e, w=w, psbc=psbc: e.matmul(
                    psbc, ones_rf[:], dn_sb[:, w * 512:(w + 1) * 512],
                    start=True, stop=True),
                    deps=[drc, prev["bcc"][hw], prev["exp"][hw], c_orf])
                bcc = k.emit("scalar", lambda e, w=w, psbc=psbc: e.copy(
                    bc_sb[:, (w % 2) * 512:(w % 2 + 1) * 512], psbc),
                    deps=[bcm, prev["nrm"][w % 2]])
                prev["exp"][hw] = bcc
                nrm = k.emit("vector", lambda e, w=w, h=h: e.tensor_tensor(
                    sa_fm[64 * (h % 2):64 * (h % 2) + 64,
                          (h // 2) * T + w * 512:(h // 2) * T + (w + 1) * 512],
                    ps_sa[0:64, w * 512:(w + 1) * 512],
                    bc_sb[:, (w % 2) * 512:(w % 2 + 1) * 512], ALU.mult),
                    deps=[bcc])
                prev["bcc"][w % 2] = bcc
                prev["nrm"][w % 2] = nrm
                hd_ops.append(nrm)
            hd_done = hd_ops[-1]
            prev["brow"] = cbr
            sa_last = hd_done

        if debug:
            d1 = k.dma("gpsimd", sa_dbg, sa_fm[:], deps=[hd_done])
            d2 = k.dma("gpsimd", v_dbg, v_tm[:], deps=[hd_done])
            k.emit("vector", lambda e: e.engine_nop(), deps=[d1, d2])
        at_sc.close()

        p5 = contextlib.ExitStack()
        wp_st = k.sb(p5, "wp_st", [128, D], F32)
        wp_b = k.sb(p5, "wp_b", [128, NCH * D], BF16)
        bp_f = k.sb(p5, "bp_f", [1, D], F32)
        bp_row = k.sb(p5, "bp_row", [1, D], BF16)
        s0_t = k.sb(p5, "s0_t", [128, 2 * D], F32)
        ldbp = k.dma("gpsimd", bp_f[:], bp_h.unsqueeze(0), deps=[hd_done])
        cbp = k.emit("vector", lambda e: e.tensor_copy(bp_row[:], bp_f[:]),
                     deps=[ldbp])
        prev_cast = None
        wp_casts = []
        for c in range(NCH):
            src = wp_h.rearrange("(c p) e -> c p e", p=128)[c]
            ld = k.dma("sync", wp_st[:], src,
                       deps=[prev_cast] if prev_cast else [hd_done, exp_any[0]])
            ca = k.emit("vector", lambda e, c=c: e.tensor_copy(
                wp_b[:, c * D:(c + 1) * D], wp_st[:]), deps=[ld])
            prev_cast = ca
            wp_casts.append(ca)
        prev_st = [None, None]
        final_sts = []
        for i in range(NTT):
            pss = [ps_a, ps_b2][i % 2]
            mm_f = None
            for c in range(NCH):
                deps0 = [wp_casts[-1], hd_done, prev["kqev"][0], prev["kqev"][1],
                         prev_st[i % 2]] if c == 0 else []
                k.emit("tensor", lambda e, i=i, c=c, pss=pss: e.matmul(
                    pss[:, 0:512],
                    sa_fm[:, c * T + i * 128: c * T + (i + 1) * 128],
                    wp_b[:, c * D: c * D + 512],
                    start=(c == 0), stop=False), deps=deps0)
                k.emit("tensor", lambda e, i=i, c=c, pss=pss: e.matmul(
                    pss[:, 512:1024],
                    sa_fm[:, c * T + i * 128: c * T + (i + 1) * 128],
                    wp_b[:, c * D + 512: c * D + 1024],
                    start=(c == 0), stop=False), deps=[])
            m1 = k.emit("tensor", lambda e, pss=pss: e.matmul(
                pss[:, 0:512], ones_row[0:1, 0:128], bp_row[:, 0:512],
                start=False, stop=True), deps=[cbp])
            m2 = k.emit("tensor", lambda e, pss=pss: e.matmul(
                pss[:, 512:1024], ones_row[0:1, 0:128], bp_row[:, 512:1024],
                start=False, stop=True), deps=[])
            ev = k.emit("scalar", lambda e, i=i, pss=pss: e.copy(
                s0_t[:, (i % 2) * D:(i % 2 + 1) * D], pss[:, 0:1024]),
                deps=[m1, m2])
            st = k.dma("gpsimd", s0_out[RT(i, 128)],
                       s0_t[:, (i % 2) * D:(i % 2 + 1) * D], deps=[ev])
            prev_st[i % 2] = st
            final_sts.append(st)
        p5.close()

        k.emit("vector", lambda e: e.engine_nop(),
               deps=[final_sts[-1], final_sts[-2], xn_writes[-1]])

        k.finalize()
    return k



def kernel(**inputs):
    raise NotImplementedError("assembled later")



def emit_ternarize(k, w_hbm, wq_scr, rows, cols, wst, absb, accb, tcol,
                   ones_rf, ones_cf, ps_tmp, sgn_b, msk_b, out_b, pre=()):
    nt = rows // 128
    acc_ops = []
    prev_abs = [None, None]
    for i in range(nt):
        b = i % 2
        wt = wst[:, b * cols:(b + 1) * cols]
        ld = k.dma("sync", wt, w_hbm[RT(i, 128)],
                   deps=(list(pre) if i < 2 else []) + [prev_abs[b]])
        ab = k.emit("scalar", lambda e, wt=wt, i=i: e.activation(
            absb[:, 0:cols], wt, AF.Abs, accum_out=accb[:, i:i + 1]), deps=[ld])
        prev_abs[b] = ab
        acc_ops.append(ab)
    red = k.emit("vector", lambda e: e.tensor_reduce(
        accb[:, nt:nt + 1], accb[:, 0:nt], AX.X, ALU.add), deps=[acc_ops[-1]])
    smm = k.emit("tensor", lambda e: e.matmul(
        ps_tmp[0:1, 0:1], ones_cf[:], accb[:, nt:nt + 1], start=True, stop=True),
        deps=[red])
    thr = k.emit("scalar", lambda e: e.activation(
        tcol[0:1, 1:2], ps_tmp[0:1, 0:1], AF.Copy, scale=0.5 / (rows * cols)),
        deps=[smm])
    bcm = k.emit("tensor", lambda e: e.matmul(
        ps_tmp[:, 1:2], ones_rf[:], tcol[0:1, 1:2], start=True, stop=True),
        deps=[thr])
    bcp = k.emit("scalar", lambda e: e.copy(tcol[:, 0:1], ps_tmp[:, 1:2]),
                 deps=[bcm])
    prev_ld = [None, None]
    prev_cm = None
    prev_wq = None
    prev_wr = None
    writes = []
    for i in range(nt):
        b = i % 2
        wt = wst[:, b * cols:(b + 1) * cols]
        ld = k.dma("sync", wt, w_hbm[RT(i, 128)],
                   deps=[acc_ops[i], prev_ld[b]])
        sg = k.emit("scalar", lambda e, wt=wt: e.activation(
            sgn_b[:, 0:cols], wt, AF.Sign), deps=[ld, prev_wq])
        ab = k.emit("scalar", lambda e, wt=wt: e.activation(
            absb[:, 0:cols], wt, AF.Abs), deps=[ld, prev_cm])
        prev_ld[b] = ab
        cm = k.emit("vector", lambda e: e.tensor_scalar(
            msk_b[:, 0:cols], absb[:, 0:cols], tcol[:, 0:1], None, ALU.is_gt),
            deps=[ab, bcp, prev_wq])
        prev_cm = cm
        wq = k.emit("vector", lambda e: e.tensor_tensor(
            out_b[:, 0:cols], sgn_b[:, 0:cols], msk_b[:, 0:cols], ALU.mult),
            deps=[cm, sg, prev_wr])
        prev_wq = wq
        wr = k.dma("gpsimd", wq_scr[RT(i, 128)], out_b[:, 0:cols], deps=[wq])
        prev_wr = wr
        writes.append(wr)
    return writes


def build_stage_b():
    k = KB("stage_b")
    x_hbm = k.dram("x", [T, D], F32, "ExternalInput")
    xn_hbm = k.dram("xn_tm", [T, D], BF16, "ExternalInput")
    s0_hbm = k.dram("s0_tm", [T, D], F32, "ExternalInput")
    wup_h = k.dram("w_up", [DFF, D], F32, "ExternalInput")
    wdn_h = k.dram("w_down", [D, DFF], F32, "ExternalInput")
    wqu_scr = k.dram("wqu", [DFF, D], BF16, "Internal")
    wqd_scr = k.dram("wqd", [D, DFF], BF16, "Internal")
    x1_out = k.dram("x1_tm", [T, D], F32, "ExternalOutput")

    with k.es:
        P = k.es
        xn_fm = k.sb(P, "xn_fm", [128, NCH * T], BF16)
        ones_col = k.sb(P, "ones_col", [128, 1], BF16)
        ones_cf = k.sb(P, "ones_cf", [128, 1], F32)
        ones_rf = k.sb(P, "ones_rf", [1, 128], F32)
        ident = k.sb(P, "ident", [128, 128], F32)
        rows_sb = k.sb(P, "rows_sb", [1, 8 * 512], F32)

        c_oc = k.emit("gpsimd", lambda e: e.memset(ones_col[:], 1.0))
        c_ocf = k.emit("gpsimd", lambda e: e.memset(ones_cf[:], 1.0))
        c_orf = k.emit("gpsimd", lambda e: e.memset(ones_rf[:], 1.0))
        mi = contextlib.ExitStack()
        iot_f = k.sb(mi, "iot_f", [128, 128], F32)
        iot_p = k.sb(mi, "iot_p", [128, 1], F32)
        c_if = k.emit("gpsimd", lambda e: e.iota(
            iot_f[:], [[1, 128]], channel_multiplier=0,
            allow_small_or_imprecise_dtypes=True))
        c_ip = k.emit("gpsimd", lambda e: e.iota(
            iot_p[:], [[1, 1]], channel_multiplier=1,
            allow_small_or_imprecise_dtypes=True))
        c_id = k.emit("vector", lambda e: e.tensor_scalar(
            ident[:], iot_f[:], iot_p[:], None, ALU.is_equal), deps=[c_if, c_ip])
        mi.close()

        xn_loads = []
        for c in range(NCH):
            xn_loads.append(k.dma("sync", xn_fm[:, c * T:(c + 1) * T],
                                  xn_hbm[:, RT(c, 128)],
                                  deps=[c_id] if c == 0 else [], transpose=True))

        tp = contextlib.ExitStack()
        wst = k.sb(tp, "wst", [128, 2 * DFF], F32)
        absb = k.sb(tp, "absb", [128, DFF], F32)
        accb = k.sb(tp, "accb", [128, 40], F32)
        tcol_u = k.sb(tp, "tcol_u", [128, 2], F32)
        tcol_d = k.sb(tp, "tcol_d", [128, 2], F32)
        sgn_b = k.sb(tp, "sgn_b", [128, DFF], BF16)
        msk_b = k.sb(tp, "msk_b", [128, DFF], BF16)
        out_b = k.sb(tp, "out_b", [128, DFF], BF16)
        pst = contextlib.ExitStack()
        ps_tmp = k.ps(pst, "ps_tmp", [128, 2])
        wr_up = emit_ternarize(k, wup_h, wqu_scr, DFF, D, wst, absb, accb,
                               tcol_u, ones_rf, ones_cf, ps_tmp, sgn_b, msk_b,
                               out_b, pre=[c_ocf, c_orf, c_id])
        wr_dn = emit_ternarize(k, wdn_h, wqd_scr, D, DFF, wst, absb, accb,
                               tcol_d, ones_rf, ones_cf, ps_tmp, sgn_b, msk_b,
                               out_b, pre=wr_up[-2:])
        tp.close()
        pst.close()

        wu = contextlib.ExitStack()
        wqT_up = k.sb(wu, "wqT_up", [128, NCH * DFF], BF16)
        wqT_loads = []
        for c in range(NCH):
            for fw in range(DFF // 512):
                wqT_loads.append(k.dma(
                    "sync", wqT_up[:, c * DFF + fw * 512: c * DFF + (fw + 1) * 512],
                    wqu_scr[RT(fw, 512), RT(c, 128)],
                    deps=(wr_up + wr_dn[-1:]) if (c == 0 and fw == 0) else [],
                    transpose=True))

        fb = contextlib.ExitStack()
        xr_fm = k.sb(fb, "xr_fm", [128, NCH * 512], BF16)
        h1 = k.sb(fb, "h1", [128, 32 * 512], BF16)
        sqb = k.sb(fb, "sqb", [128, 512], BF16)
        wdT = k.sb(fb, "wdT", [128, 2 * 1024], BF16)
        g_t = k.sb(fb, "g_t", [128, D], F32)
        x_t = k.sb(fb, "x_t", [128, D], F32)
        s0_t = k.sb(fb, "s0_t", [128, D], F32)
        sg_t = k.sb(fb, "sg_t", [128, D], F32)
        r2c = k.sb(fb, "r2c", [128, 1], F32)
        psf = contextlib.ExitStack()
        ps_1 = k.ps(psf, "ps_1", [128, 512])
        ps_2 = k.ps(psf, "ps_2", [128, 2 * 512])
        ps_r = k.ps(psf, "ps_r", [1, 512])
        ps_g = k.ps(psf, "ps_g", [128, 1024])
        ps_c = k.ps(psf, "ps_c", [128, 2])

        e8 = 1e-8
        prev = {"sq": None, "h1ev": [None, None], "rowrd": [None],
                "wdmm": [None, None], "gev": None, "x1wr": [None, None],
                "r2c": None, "comb": None, "xrrd": None, "bcmrd": None}
        last_x1_wr = None
        for w in range(4):
            ro = 0
            smm = None
            for c in range(NCH):
                sq = k.emit("vector", lambda e, c=c, w=w: e.tensor_tensor(
                    sqb[:], xn_fm[:, c * T + w * 512: c * T + (w + 1) * 512],
                    xn_fm[:, c * T + w * 512: c * T + (w + 1) * 512], ALU.mult),
                    deps=[xn_loads[-1], prev["sq"]])
                smm = k.emit("tensor", lambda e, c=c, ro=ro: e.matmul(
                    ps_r[0:1, ro:ro + 512], ones_col[:], sqb[:],
                    start=(c == 0), stop=(c == NCH - 1)),
                    deps=[sq, c_oc] + ([prev["rowrd"][0]] if c == 0 else []))
                prev["sq"] = smm
            r_sl = rows_sb[:, w * 512: w * 512 + 512]
            cp1 = k.emit("scalar", lambda e, ro=ro, r_sl=r_sl: e.activation(
                r_sl, ps_r[0:1, ro:ro + 512], AF.Ln), deps=[smm])
            cp2 = k.emit("scalar", lambda e, r_sl=r_sl: e.activation(
                r_sl, r_sl, AF.Exp, scale=0.5), deps=[cp1])
            cp3 = k.emit("vector", lambda e, r_sl=r_sl: e.tensor_scalar(
                r_sl, r_sl, 1.0 / 32.0, e8, ALU.mult, ALU.add), deps=[cp2])
            cp4 = k.emit("vector", lambda e, r_sl=r_sl: e.reciprocal(
                r_sl, r_sl), deps=[cp3])
            bcm = k.emit("tensor", lambda e, r_sl=r_sl: e.matmul(
                ps_1[:], ones_rf[:], r_sl,
                start=True, stop=True), deps=[cp4, c_orf, prev["bcmrd"]])
            xr_ops = []
            for c in range(NCH):
                xr = k.emit("vector", lambda e, c=c, w=w, ro=ro: e.tensor_tensor(
                    xr_fm[:, c * 512:(c + 1) * 512],
                    xn_fm[:, c * T + w * 512: c * T + (w + 1) * 512],
                    ps_1[:], ALU.mult),
                    deps=[bcm] + ([prev["h1ev"][0], prev["h1ev"][1]] if c == 0 else []))
                xr_ops.append(xr)
            prev["bcmrd"] = xr_ops[-1]
            s1_evs = []
            for ft in range(32):
                ip = ft % 2
                ps = ps_2[:, ip * 512:(ip + 1) * 512]
                mm = None
                for c in range(NCH):
                    mm = k.emit("tensor", lambda e, c=c, ft=ft, ps=ps: e.matmul(
                        ps, wqT_up[:, c * DFF + ft * 128: c * DFF + (ft + 1) * 128],
                        xr_fm[:, c * 512:(c + 1) * 512],
                        start=(c == 0), stop=(c == NCH - 1)),
                        deps=([wqT_loads[-1], xr_ops[-1], prev["h1ev"][ip]]
                              if c == 0 else []))
                ev = k.emit("scalar", lambda e, ft=ft, ps=ps: e.activation(
                    h1[:, ft * 512:(ft + 1) * 512], ps, AF.Gelu),
                    deps=[mm, prev["gev"]])
                prev["h1ev"][ip] = ev
                s1_evs.append(ev)
            smm2 = None
            for ft in range(32):
                sq = k.emit("vector", lambda e, ft=ft: e.tensor_tensor(
                    sqb[:], h1[:, ft * 512:(ft + 1) * 512],
                    h1[:, ft * 512:(ft + 1) * 512], ALU.mult),
                    deps=[s1_evs[ft], prev["sq"]])
                smm2 = k.emit("tensor", lambda e, ft=ft, ro=ro: e.matmul(
                    ps_r[0:1, ro:ro + 512], ones_col[:], sqb[:],
                    start=(ft == 0), stop=(ft == 31)),
                    deps=[sq] + ([cp1] if ft == 0 else []))
                prev["sq"] = smm2
            r2_sl = rows_sb[:, 2048 + w * 512: 2048 + (w + 1) * 512]
            cq1 = k.emit("scalar", lambda e, r2_sl=r2_sl, ro=ro: e.activation(
                r2_sl, ps_r[0:1, ro:ro + 512], AF.Ln), deps=[smm2])
            cq2 = k.emit("scalar", lambda e, r2_sl=r2_sl: e.activation(
                r2_sl, r2_sl, AF.Exp, scale=0.5), deps=[cq1])
            cq3 = k.emit("vector", lambda e, r2_sl=r2_sl: e.tensor_scalar(
                r2_sl, r2_sl, 1.0 / 64.0, e8, ALU.mult, ALU.add), deps=[cq2])
            cq4 = k.emit("vector", lambda e, r2_sl=r2_sl: e.reciprocal(
                r2_sl, r2_sl), deps=[cq3])
            prev["rowrd"][0] = cq1
            for tt in range(4):
                tau = w * 4 + tt
                trm = k.emit("tensor", lambda e, r2_sl=r2_sl, tt=tt: e.transpose(
                    ps_c[:, 0:1], r2_sl[:, tt * 128:(tt + 1) * 128], ident[0:1, 0:1]),
                    deps=[cq4, c_id, prev["r2c"]])
                rcc = k.emit("scalar", lambda e: e.copy(r2c[:], ps_c[:, 0:1]),
                             deps=[trm])
                prev["r2c"] = rcc
                ipg = 0
                mm = None
                for ft in range(32):
                    ld = k.dma("sync", wdT[:, (ft % 2) * 1024:(ft % 2 + 1) * 1024],
                               wqd_scr[0:D, RT(ft, 128)],
                               deps=[prev["wdmm"][ft % 2]], transpose=True)
                    for half in range(2):
                        mm = k.emit("tensor",
                                    lambda e, ft=ft, tt=tt, half=half, ipg=ipg: e.matmul(
                                        ps_g[:, ipg * 1024 + half * 512:
                                             ipg * 1024 + (half + 1) * 512],
                                        h1[:, ft * 512 + tt * 128: ft * 512 + tt * 128 + 128],
                                        wdT[:, (ft % 2) * 1024 + half * 512:
                                            (ft % 2) * 1024 + (half + 1) * 512],
                                        start=(ft == 0), stop=(ft == 31)),
                                    deps=[ld, s1_evs[-1]] +
                                    ([prev["gev"]] if (ft == 0 and half == 0) else []))
                    prev["wdmm"][ft % 2] = mm
                ldx = k.dma("gpsimd", x_t[:], x_hbm[RT(tau, 128)],
                            deps=[prev["x1wr"][tau % 2]])
                lds = k.dma("gpsimd", s0_t[:], s0_hbm[RT(tau, 128)],
                            deps=[])
                gv = k.emit("scalar", lambda e, ipg=ipg: e.activation(
                    g_t[:, 0:512], ps_g[:, ipg * 1024:ipg * 1024 + 512],
                    AF.Silu, scale=r2c[:]), deps=[mm, rcc, prev["comb"]])
                gv2 = k.emit("scalar", lambda e, ipg=ipg: e.activation(
                    g_t[:, 512:1024], ps_g[:, ipg * 1024 + 512:ipg * 1024 + 1024],
                    AF.Silu, scale=r2c[:]), deps=[gv])
                prev["gev"] = gv2
                sg = k.emit("vector", lambda e: e.tensor_tensor(
                    sg_t[:], s0_t[:], g_t[:], ALU.mult), deps=[gv2, lds])
                x1v = k.emit("vector", lambda e: e.tensor_tensor(
                    sg_t[:], sg_t[:], x_t[:], ALU.add), deps=[sg, ldx])
                prev["comb"] = x1v
                wrx = k.dma("gpsimd", x1_out[RT(tau, 128)], sg_t[:], deps=[x1v])
                prev["x1wr"][tau % 2] = wrx
                last_x1_wr = wrx

        fb.close()
        psf.close()
        wu.close()
        k.emit("vector", lambda e: e.engine_nop(), deps=[last_x1_wr])
        k.finalize()
    return k



def build_stage_c():
    k = KB("stage_c")
    x1_hbm = k.dram("x1_tm", [T, D], F32, "ExternalInput")
    alog_h = k.dram("A_log", [D, NST], F32, "ExternalInput")
    dp_h = k.dram("Dp", [D], F32, "ExternalInput")
    wxp_h = k.dram("W_xproj", [D, D + 2 * NST], F32, "ExternalInput")
    bxp_h = k.dram("b_xproj", [D + 2 * NST], F32, "ExternalInput")
    wout_h = k.dram("W_out", [D, D], F32, "ExternalInput")
    bout_h = k.dram("b_out", [D], F32, "ExternalInput")
    scr_B = k.dram("scrB", [NST, T], F32, "Internal")
    scr_C = k.dram("scrC", [T, NST], BF16, "Internal")
    xn1_out = k.dram("xn1_tm", [T, D], BF16, "ExternalOutput")
    s1_out = k.dram("s1_tm", [T, D], F32, "ExternalOutput")

    EHX = D + 2 * NST

    with k.es:
        P = k.es
        k.kcol(1e-5)
        one_ap, one_op = k.kcol(1.0)
        xn_fm = k.sb(P, "xn_fm", [128, NCH * T], BF16)
        ones_row = k.sb(P, "ones_row", [1, 512], BF16)
        ones_rf = k.sb(P, "ones_rf", [1, 128], F32)
        ident1 = k.sb(P, "ident1", [1, 1], F32)
        ident16 = k.sb(P, "ident16", [16, 16], BF16)
        c_or = k.emit("gpsimd", lambda e: e.memset(ones_row[:], 1.0))
        c_orf = k.emit("gpsimd", lambda e: e.memset(ones_rf[:], 1.0))
        c_i1 = k.emit("gpsimd", lambda e: e.memset(ident1[:], 1.0))
        mi = contextlib.ExitStack()
        iot_f = k.sb(mi, "iot_f", [16, 16], F32)
        iot_p = k.sb(mi, "iot_p", [16, 1], F32)
        c_if = k.emit("gpsimd", lambda e: e.iota(
            iot_f[:], [[1, 16]], channel_multiplier=0,
            allow_small_or_imprecise_dtypes=True))
        c_ip = k.emit("gpsimd", lambda e: e.iota(
            iot_p[:], [[1, 1]], channel_multiplier=1,
            allow_small_or_imprecise_dtypes=True))
        c_id16 = k.emit("vector", lambda e: e.tensor_scalar(
            ident16[:], iot_f[:], iot_p[:], None, ALU.is_equal), deps=[c_if, c_ip])
        mi.close()

        ln_sc = contextlib.ExitStack()
        xn_writes, ln_last = emit_ln_tm(k, ln_sc, x1_hbm, xn1_out,
                                        war=[c_id16, c_i1])
        ln_sc.close()
        xn_loads = []
        for c in range(NCH):
            xn_loads.append(k.dma("sync", xn_fm[:, c * T:(c + 1) * T],
                                  xn1_out[:, RT(c, 128)],
                                  deps=xn_writes if c == 0 else [], transpose=True))

        wsc = contextlib.ExitStack()
        wxp_b = k.sb(wsc, "wxp_b", [128, NCH * EHX], BF16)
        wout_b = k.sb(wsc, "wout_b", [128, NCH * D], BF16)
        A_sb = k.sb(wsc, "A_sb", [128, 128], F32)
        dp_col = k.sb(wsc, "dp_col", [128, NCH], F32)
        bxp_b = k.sb(wsc, "bxp_b", [1, EHX], BF16)
        bout_b = k.sb(wsc, "bout_b", [1, D], BF16)
        stg = contextlib.ExitStack()
        dp_row = k.sb(stg, "dp_row", [1, D], F32)
        bxp_f = k.sb(stg, "bxp_f", [1, EHX], F32)
        bout_f = k.sb(stg, "bout_f", [1, D], F32)
        wstg = k.sb(stg, "wstg", [128, EHX], F32)
        pw = contextlib.ExitStack()
        ps_w = k.ps(pw, "ps_w", [128, 2])

        prev_c = None
        wx_casts = []
        for c in range(NCH):
            ld = k.dma("sync", wstg[:], wxp_h[RT(c, 128)],
                       deps=(list(ln_last) if c == 0 else [prev_c]))
            ca = k.emit("vector", lambda e, c=c: e.tensor_copy(
                wxp_b[:, c * EHX:(c + 1) * EHX], wstg[:]), deps=[ld])
            prev_c = ca
            wx_casts.append(ca)
        for c in range(NCH):
            ld = k.dma("sync", wstg[:, 0:D], wout_h[RT(c, 128)], deps=[prev_c])
            ca = k.emit("vector", lambda e, c=c: e.tensor_copy(
                wout_b[:, c * D:(c + 1) * D], wstg[:, 0:D]), deps=[ld])
            prev_c = ca
        wcast_last = prev_c
        lda = k.dma("sync", A_sb[:].rearrange("p (c n) -> p c n", c=NCH),
                    alog_h.rearrange("(c p) n -> p c n", p=128), deps=[])
        a_exp = k.emit("scalar", lambda e: e.activation(A_sb[:], A_sb[:], AF.Exp),
                       deps=[lda])
        a_neg = k.emit("vector", lambda e: e.tensor_scalar(
            A_sb[:], A_sb[:], -1.0, None, ALU.mult), deps=[a_exp])
        ldp = k.dma("sync", dp_row[:], dp_h.unsqueeze(0), deps=[])
        dp_ops = []
        for c in range(NCH):
            tr = k.emit("tensor", lambda e, c=c: e.transpose(
                ps_w[:, 0:1], dp_row[0:1, c * 128:(c + 1) * 128], ident1[:]),
                deps=[ldp, c_i1] + dp_ops[-1:])
            cp = k.emit("scalar", lambda e, c=c: e.copy(dp_col[:, c:c + 1],
                                                        ps_w[:, 0:1]), deps=[tr])
            dp_ops.append(cp)
        ldbx = k.dma("sync", bxp_f[:], bxp_h.unsqueeze(0), deps=[])
        cbx = k.emit("vector", lambda e: e.tensor_copy(bxp_b[:], bxp_f[:]),
                     deps=[ldbx])
        ldbo = k.dma("sync", bout_f[:], bout_h.unsqueeze(0), deps=[])
        cbo = k.emit("vector", lambda e: e.tensor_copy(bout_b[:], bout_f[:]),
                     deps=[ldbo])

        stg.close()
        mb = contextlib.ExitStack()
        delta_w = k.sb(mb, "delta_w", [128, NCH * 512], F32)
        u_w = k.sb(mb, "u_w", [128, NCH * 512], F32)
        t1 = k.sb(mb, "t1", [128, 512], F32)
        B_nat = k.sb(mb, "B_nat", [16, 512], F32)
        C_tmb = k.sb(mb, "C_tmb", [128, 4 * 16], BF16)
        C_f16 = k.sb(mb, "C_f16", [16, 512], BF16)
        H_c = k.sb(mb, "H_c", [128, 512 * NST], F32)
        hc_b = k.sb(mb, "hc_b", [128, 512 * NST], BF16)
        C_rep = k.sb(mb, "C_rep", [128, 512 * NST], BF16)
        B_bc = k.sb(mb, "B_bc", [128, 2 * 512], F32)
        ab_b = k.sb(mb, "ab_b", [128, 2 * 512], F32)
        bx_b = k.sb(mb, "bx_b", [128, 2 * 512], F32)
        y_w = k.sb(mb, "y_w", [128, NCH * 512], BF16)
        state = k.sb(mb, "state", [128, NCH * NST], F32)
        s1_t = k.sb(mb, "s1_t", [128, 2 * D], F32)
        pm = contextlib.ExitStack()
        ps_p = k.ps(pm, "ps_p", [128, 2 * 512])
        ps_bc1 = k.ps(pm, "ps_bc1", [16, 512])
        ps_bc2 = k.ps(pm, "ps_bc2", [16, 512])
        ps_ct = k.ps(pm, "ps_ct", [128, 16], BF16)
        ps_s = k.ps(pm, "ps_s", [128, 1024])

        st0 = k.emit("gpsimd", lambda e: e.memset(state[:], 0.0),
                     deps=[cbx, cbo, dp_ops[-1], wcast_last, a_neg])
        state_ops = [st0] * NCH

        prev = {"dl": [None, None], "t1": None, "Bwr": None, "Cwr": None,
                "Bev": None, "Cev": [None], "Cf": None,
                "ab": [None, None], "bx": [None, None],
                "Bld": [None, None], "scan": [None, None], "hc": None,
                "red": None, "yst": None, "s1wr": [None, None], "s1ev": None,
                "crep": None, "proj": None, "uw": None, "hcrd": None}
        last_s1_wr = None
        for w in range(4):
            sp_ops = []
            for mt in range(NCH):
                ip = mt % 2
                ps = ps_p[:, ip * 512:(ip + 1) * 512]
                mm = None
                for c in range(NCH):
                    mm = k.emit("tensor", lambda e, c=c, mt=mt, w=w, ps=ps: e.matmul(
                        ps, wxp_b[:, c * EHX + mt * 128: c * EHX + (mt + 1) * 128],
                        xn_fm[:, c * T + w * 512: c * T + (w + 1) * 512],
                        start=(c == 0), stop=False),
                        deps=([wcast_last, xn_loads[-1], prev["dl"][ip]]
                              if c == 0 else []))
                mm = k.emit("tensor", lambda e, mt=mt, ps=ps: e.matmul(
                    ps, bxp_b[:, mt * 128:(mt + 1) * 128], ones_row[:],
                    start=False, stop=True), deps=[cbx, c_or])
                ab1 = k.emit("scalar", lambda e, ps=ps: e.activation(
                    t1[:], ps, AF.Abs), deps=[mm, prev["t1"]])
                ex1 = k.emit("scalar", lambda e: e.activation(
                    t1[:], t1[:], AF.Exp, scale=-1.0), deps=[ab1])
                ln1 = k.emit("scalar", lambda e: e.activation(
                    t1[:], t1[:], AF.Ln, bias=one_ap), deps=[ex1, one_op])
                dl = k.emit("vector", lambda e, mt=mt, ps=ps: e.scalar_tensor_tensor(
                    delta_w[:, mt * 512:(mt + 1) * 512], ps, 0.0, t1[:],
                    ALU.max, ALU.add), deps=[ln1])
                prev["t1"] = dl
                prev["dl"][ip] = dl
                sp_ops.append(dl)
            mmB = None
            mmC = None
            for c in range(NCH):
                mmB = k.emit("tensor", lambda e, c=c, w=w: e.matmul(
                    ps_bc1[:], wxp_b[:, c * EHX + D: c * EHX + D + 16],
                    xn_fm[:, c * T + w * 512: c * T + (w + 1) * 512],
                    start=(c == 0), stop=False), deps=[prev["Bev"]] if c == 0 else [])
                mmC = k.emit("tensor", lambda e, c=c, w=w: e.matmul(
                    ps_bc2[:], wxp_b[:, c * EHX + D + 16: c * EHX + D + 32],
                    xn_fm[:, c * T + w * 512: c * T + (w + 1) * 512],
                    start=(c == 0), stop=False), deps=[prev["Cev"][0]] if c == 0 else [])
            mmB = k.emit("tensor", lambda e: e.matmul(
                ps_bc1[:], bxp_b[:, D:D + 16], ones_row[:],
                start=False, stop=True), deps=[])
            mmC = k.emit("tensor", lambda e: e.matmul(
                ps_bc2[:], bxp_b[:, D + 16:D + 32], ones_row[:],
                start=False, stop=True), deps=[])
            bev = k.emit("vector", lambda e: e.tensor_copy(B_nat[:], ps_bc1[:]),
                         deps=[mmB, prev["Bwr"]])
            prev["Bev"] = bev
            bwr = k.dma("gpsimd", scr_B[:, w * 512:(w + 1) * 512], B_nat[:],
                        deps=[bev])
            prev["Bwr"] = bwr
            cf16 = k.emit("scalar", lambda e: e.copy(C_f16[:], ps_bc2[:]),
                          deps=[mmC, prev["Cf"]])
            cev_ops = []
            for tt in range(4):
                trc = k.emit("tensor", lambda e, tt=tt: e.transpose(
                    ps_ct[:], C_f16[:, tt * 128:(tt + 1) * 128], ident16[:]),
                    deps=[cf16, prev["Cwr"], c_id16] + cev_ops[-1:])
                cev = k.emit("scalar", lambda e, tt=tt: e.copy(
                    C_tmb[:, tt * 16:(tt + 1) * 16], ps_ct[:]), deps=[trc])
                cev_ops.append(cev)
            prev["Cev"][0] = cf16
            prev["Cf"] = cev_ops[-1]
            cwr = k.dma("gpsimd",
                        scr_C[w * 512:(w + 1) * 512, :].rearrange(
                            "(tt p) n -> p tt n", p=128),
                        C_tmb[:].rearrange("p (tt n) -> p tt n", tt=4),
                        deps=cev_ops)
            prev["Cwr"] = cwr
            crep = k.dma("gpsimd", C_rep[:],
                         scr_C[w * 512:(w + 1) * 512, :].rearrange(
                             "t n -> (t n)").unsqueeze(0).partition_broadcast(
                             128).squeeze(1),
                         deps=[cwr, prev["crep"]])
            u_ops = []
            for c in range(NCH):
                uo = k.emit("vector", lambda e, c=c, w=w: e.tensor_tensor(
                    u_w[:, c * 512:(c + 1) * 512],
                    delta_w[:, c * 512:(c + 1) * 512],
                    xn_fm[:, c * T + w * 512: c * T + (w + 1) * 512], ALU.mult),
                    deps=[sp_ops[c], prev["uw"]])
                u_ops.append(uo)
            for c in range(NCH):
                for n in range(NST):
                    ib = n % 2
                    bld = k.dma("gpsimd", B_bc[:, ib * 512:(ib + 1) * 512],
                                scr_B[n, w * 512:(w + 1) * 512].unsqueeze(
                                    0).partition_broadcast(128).squeeze(1),
                                deps=[bwr, prev["bx"][ib]] if (c == 0 and n < 2)
                                else [prev["bx"][ib]])
                    abv = k.emit("scalar", lambda e, c=c, n=n, ib=ib: e.activation(
                        ab_b[:, ib * 512:(ib + 1) * 512],
                        delta_w[:, c * 512:(c + 1) * 512], AF.Exp,
                        scale=A_sb[:, c * 16 + n: c * 16 + n + 1]),
                        deps=[sp_ops[c], a_neg, prev["ab"][ib]])
                    bxv = k.emit("vector", lambda e, c=c, ib=ib: e.tensor_tensor(
                        bx_b[:, ib * 512:(ib + 1) * 512],
                        u_w[:, c * 512:(c + 1) * 512],
                        B_bc[:, ib * 512:(ib + 1) * 512], ALU.mult),
                        deps=[u_ops[c], bld, prev["scan"][ib]])
                    hsl = H_c[:].rearrange("p (t n) -> p t n", n=NST)[:, :, n]
                    sc = k.emit("vector",
                                lambda e, c=c, n=n, ib=ib, hsl=hsl: e.tensor_tensor_scan(
                                    hsl, ab_b[:, ib * 512:(ib + 1) * 512],
                                    bx_b[:, ib * 512:(ib + 1) * 512],
                                    state[:, c * NST + n: c * NST + n + 1],
                                    ALU.mult, ALU.add),
                                deps=[abv, bxv, state_ops[c]] +
                                ([prev["hcrd"]] if n == 0 else []))
                    prev["ab"][ib] = sc
                    prev["bx"][ib] = sc
                    prev["scan"][ib] = sc
                sts = k.emit("vector", lambda e, c=c: e.tensor_copy(
                    state[:, c * NST:(c + 1) * NST],
                    H_c[:, 511 * NST:512 * NST]), deps=[prev["scan"][0],
                                                        prev["scan"][1]])
                state_ops[c] = sts
                hc = k.emit("vector", lambda e: e.tensor_tensor(
                    hc_b[:], H_c[:], C_rep[:], ALU.mult),
                    deps=[sts, crep, prev["red"]])
                prev["hc"] = hc
                red = k.emit("vector", lambda e: e.tensor_reduce(
                    t1[:], hc_b[:].rearrange("p (t n) -> p t n", n=NST),
                    AX.X, ALU.add), deps=[hc, prev["yst"]])
                prev["red"] = red
                yst = k.emit("vector", lambda e, c=c, w=w: e.scalar_tensor_tensor(
                    y_w[:, c * 512:(c + 1) * 512],
                    xn_fm[:, c * T + w * 512: c * T + (w + 1) * 512],
                    dp_col[:, c:c + 1], t1[:], ALU.mult, ALU.add),
                    deps=[red, dp_ops[-1], prev["s1ev"]])
                prev["yst"] = yst
                prev["hcrd"] = yst
            prev["uw"] = prev["hcrd"]
            prev["t1"] = prev["yst"]
            prev["crep"] = prev["hc"]
            for tt in range(4):
                tau = w * 4 + tt
                mms = []
                for c in range(NCH):
                    for half in range(2):
                        mm = k.emit("tensor",
                                    lambda e, c=c, tt=tt, half=half: e.matmul(
                                        ps_s[:, half * 512:(half + 1) * 512],
                                        y_w[:, c * 512 + tt * 128: c * 512 + tt * 128 + 128],
                                        wout_b[:, c * D + half * 512:
                                               c * D + (half + 1) * 512],
                                        start=(c == 0), stop=False),
                                    deps=([prev["yst"], prev["s1ev"]]
                                          if (c == 0 and half == 0) else []))
                m1 = k.emit("tensor", lambda e: e.matmul(
                    ps_s[:, 0:512], ones_row[0:1, 0:128], bout_b[:, 0:512],
                    start=False, stop=True), deps=[cbo])
                m2 = k.emit("tensor", lambda e: e.matmul(
                    ps_s[:, 512:1024], ones_row[0:1, 0:128], bout_b[:, 512:1024],
                    start=False, stop=True), deps=[])
                ev = k.emit("scalar", lambda e, tt=tt: e.copy(
                    s1_t[:, (tt % 2) * D:(tt % 2 + 1) * D], ps_s[:]),
                    deps=[m1, m2, prev["s1wr"][tt % 2]])
                prev["s1ev"] = ev
                wr = k.dma("gpsimd", s1_out[RT(tau, 128)],
                           s1_t[:, (tt % 2) * D:(tt % 2 + 1) * D], deps=[ev])
                prev["s1wr"][tt % 2] = wr
                last_s1_wr = wr
        mb.close()
        pm.close()
        wsc.close()
        k.emit("vector", lambda e: e.engine_nop(), deps=[last_s1_wr, xn_writes[-1]])
        k.finalize()
    return k



def build_stage_d(debug=False):
    k = KB("stage_d")
    NHR = 4
    FF1 = 16384
    x1_hbm = k.dram("x1_tm", [T, D], F32, "ExternalInput")
    xn_hbm = k.dram("xn1_tm", [T, D], BF16, "ExternalInput")
    s1_hbm = k.dram("s1_tm", [T, D], F32, "ExternalInput")
    mk_h = k.dram("rail_maker", [NHR, 64, 16, 16], F32, "ExternalInput")
    up_h = k.dram("rail_up_proj", [NHR, 64, 64, 16], F32, "ExternalInput")
    wcp_h = k.dram("w_cproj", [D, FF1], F32, "ExternalInput")
    scr_mk = k.dram("scr_mk", [NHR * 64 * 16, 16], BF16, "Internal")
    scr_up = k.dram("scr_up", [NHR * 64 * 64, 16], BF16, "Internal")
    scr_M = k.dram("scr_M", [256, 16, 64], BF16, "Internal")
    scr_cp = k.dram("scr_cp", [D, FF1], BF16, "Internal")
    out_h = k.dram("out", [T, D], F32, "ExternalOutput")
    if debug:
        bdm_dbg = k.dram("bdm_dbg", [128, 128 * 128], BF16, "ExternalOutput")
        h_dbg = k.dram("h_dbg", [128, 2 * 256], BF16, "ExternalOutput")
        mkT_dbg = k.dram("mkT_dbg", [16, 32 * 128], BF16, "ExternalOutput")
        upT_dbg = k.dram("upT_dbg", [16, 32 * 512], BF16, "ExternalOutput")
        mst_dbg = k.dram("mst_dbg", [16, 256 * 64], BF16, "ExternalOutput")

    with k.es:
        P = k.es
        xn_fm = k.sb(P, "xn_fm", [128, NCH * T], BF16)
        BDM = k.sb(P, "BDM", [128, 128 * 128], BF16)
        xn_loads = []
        for c in range(NCH):
            xn_loads.append(k.dma("sync", xn_fm[:, c * T:(c + 1) * T],
                                  xn_hbm[:, RT(c, 128)], transpose=True))

        rp = contextlib.ExitStack()
        mk_f = k.sb(rp, "mk_f", [128, 512], F32)
        up_f = k.sb(rp, "up_f", [128, 2048], F32)
        mk_b = k.sb(rp, "mk_b", [128, 512], BF16)
        up_b = k.sb(rp, "up_b", [128, 2048], BF16)
        mkT = k.sb(rp, "mkT", [16, 32 * 128], BF16)
        upT = k.sb(rp, "upT", [16, 32 * 512], BF16)
        Mst = k.sb(rp, "Mst", [16, 256 * 64], BF16)
        prp = contextlib.ExitStack()
        ps_m = k.ps(prp, "ps_m", [16, 64])

        identb = k.sb(rp, "identb", [128, 128], BF16)
        mi = contextlib.ExitStack()
        iot_f = k.sb(mi, "iot_f", [128, 128], F32)
        iot_p = k.sb(mi, "iot_p", [128, 1], F32)
        c_if = k.emit("gpsimd", lambda e: e.iota(
            iot_f[:], [[1, 128]], channel_multiplier=0,
            allow_small_or_imprecise_dtypes=True))
        c_ip = k.emit("gpsimd", lambda e: e.iota(
            iot_p[:], [[1, 1]], channel_multiplier=1,
            allow_small_or_imprecise_dtypes=True))
        c_idb = k.emit("vector", lambda e: e.tensor_scalar(
            identb[:], iot_f[:], iot_p[:], None, ALU.is_equal), deps=[c_if, c_ip])
        mi.close()
        ldm = k.dma("sync", mk_f[:].rearrange("p (t j) -> p t j", j=16),
                    mk_h.rearrange("h r i j -> (h r i) j").rearrange(
                        "(t p) j -> p t j", p=128), deps=[c_idb])
        cm1 = k.emit("vector", lambda e: e.tensor_copy(mk_b[:], mk_f[:]),
                     deps=[ldm])
        ldu = k.dma("sync", up_f[:].rearrange("p (t j) -> p t j", j=16),
                    up_h.rearrange("h r q j -> (h r q) j").rearrange(
                        "(t p) j -> p t j", p=128))
        cu1 = k.emit("vector", lambda e: e.tensor_copy(up_b[:], up_f[:]),
                     deps=[ldu])
        pst = contextlib.ExitStack()
        ps_t16 = k.ps(pst, "ps_t16", [16, 128], BF16)
        tmk = []
        tup = []
        prev_t = None
        for h in range(NHR):
            for c in range(NCH):
                g = h * NCH + c
                t0 = h * 8 + c
                tr = k.emit("tensor", lambda e, t0=t0: e.transpose(
                    ps_t16[:], mk_b[:, t0 * 16:(t0 + 1) * 16], identb[:]),
                    deps=[cm1, c_idb, prev_t])
                cpy = k.emit("scalar", lambda e, g=g: e.copy(
                    mkT[:, g * 128:(g + 1) * 128], ps_t16[:]), deps=[tr])
                prev_t = cpy
                tmk.append(cpy)
                for s in range(4):
                    t1 = h * 32 + c * 4 + s
                    tr = k.emit("tensor", lambda e, t1=t1: e.transpose(
                        ps_t16[:], up_b[:, t1 * 16:(t1 + 1) * 16], identb[:]),
                        deps=[cu1, prev_t])
                    cpy = k.emit("scalar", lambda e, g=g, s=s: e.copy(
                        upT[:, g * 512 + s * 128: g * 512 + (s + 1) * 128],
                        ps_t16[:]), deps=[tr])
                    prev_t = cpy
                    tup.append(cpy)
        pst.close()
        mev = []
        prev_ev = None
        for h in range(NHR):
            for c in range(NCH):
                g = h * NCH + c
                for e2 in range(8):
                    gi = (h * NCH + c) * 8 + e2
                    mm = k.emit("tensor", lambda e, g=g, e2=e2: e.matmul(
                        ps_m[:], mkT[:, g * 128 + e2 * 16: g * 128 + (e2 + 1) * 16],
                        upT[:, g * 512 + e2 * 64: g * 512 + (e2 + 1) * 64],
                        start=True, stop=True),
                        deps=[tmk[-1], tup[-1], prev_ev] if gi == 0 or prev_ev
                        else [tmk[-1], tup[-1]])
                    ev = k.emit("scalar", lambda e, gi=gi: e.copy(
                        Mst[:, gi * 64:(gi + 1) * 64], ps_m[:]), deps=[mm])
                    prev_ev = ev
                    mev.append(ev)
        if debug:
            dd1 = k.dma("gpsimd", mkT_dbg, mkT[:], deps=[tmk[-1]])
            dd2 = k.dma("gpsimd", upT_dbg, upT[:], deps=[tup[-1]])
            dd3 = k.dma("gpsimd", mst_dbg, Mst[:], deps=[mev[-1]])
            if debug == 2:
                k.emit("vector", lambda e: e.engine_nop(), deps=[dd1, dd2, dd3])
                rp.close()
                prp.close()
                k.finalize()
                return k
        wrM = k.dma("gpsimd", scr_M.rearrange("g i k -> i g k"),
                    Mst[:].rearrange("i (g k) -> i g k", k=64), deps=[mev[-1]])
        zb = k.emit("gpsimd", lambda e: e.memset(BDM[:], 0.0), deps=[])
        asm = []
        for c in range(NCH):
            for h in range(NHR):
                for q in range(4):
                    tb = ((c * NHR + h) * 4 + q) * 128
                    for e2 in range(8):
                        g = (h * NCH + c) * 8 + e2
                        asm.append(k.dma(
                            "gpsimd",
                            BDM[:].rearrange("p f -> p f")[
                                16 * e2:16 * (e2 + 1),
                                tb + 16 * e2: tb + 16 * (e2 + 1)],
                            scr_M[g, :, q * 16:(q + 1) * 16],
                            deps=[wrM, zb] if not asm else []))
        rp.close()
        prp.close()

        if debug:
            dbb = k.dma("gpsimd", bdm_dbg, BDM[:], deps=[asm[-1]])
            if debug == 3:
                k.emit("vector", lambda e: e.engine_nop(), deps=[dbb])
                rp.close()
                prp.close()
                k.finalize()
                return k
        cp = contextlib.ExitStack()
        cst = k.sb(cp, "cst", [128, 2 * 2048], F32)
        cst2 = k.sb(cp, "cst2", [128, 2048], BF16)
        prev_cast = [None, None]
        prev_wr = None
        cp_writes = []
        for dt in range(8):
            for W in range(8):
                ib = (dt * 8 + W) % 2
                H = W // 2
                ld = k.dma("sync", cst[:, ib * 2048:(ib + 1) * 2048],
                           wcp_h[RT(dt, 128), W * 2048:(W + 1) * 2048],
                           deps=[prev_cast[ib]] if prev_cast[ib] else [wrM])
                src = cst[:, ib * 2048:(ib + 1) * 2048].rearrange(
                    "p (c e q l) -> p c q e l", c=4, e=8, q=4)
                dst = cst2[:].rearrange("p (c q e l) -> p c q e l", c=4, q=4, e=8)
                ca = k.emit("vector", lambda e, src=src, dst=dst: e.tensor_copy(
                    dst, src), deps=[ld, prev_wr])
                prev_cast[ib] = ca
                chalf = W % 2
                dstd = scr_cp[RT(dt, 128)].rearrange(
                    "p (cc hh q r) -> p cc hh q r", cc=8, hh=4, q=4)[
                    :, 4 * chalf:4 * chalf + 4, H]
                wr = k.dma("gpsimd", dstd,
                           cst2[:].rearrange("p (c q r) -> p c q r", c=4, q=4),
                           deps=[ca])
                prev_wr = wr
                cp_writes.append(wr)
        cp.close()

        mn = contextlib.ExitStack()
        h_fc = k.sb(mn, "h_fc", [128, 2 * 256], BF16)
        wcT = k.sb(mn, "wcT", [128, 2 * 1024], BF16)
        g_t = k.sb(mn, "g_t", [128, D], F32)
        x1_t = k.sb(mn, "x1_t", [128, D], F32)
        s1_t = k.sb(mn, "s1_t", [128, D], F32)
        o_t = k.sb(mn, "o_t", [128, 2 * D], F32)
        pmn = contextlib.ExitStack()
        ps_z = k.ps(pmn, "ps_z", [128, 2 * 256])
        ps_g = k.ps(pmn, "ps_g", [128, 4 * 512])

        prev = {"zev": [None, None], "wld": [None, None], "cpmm": [None, None],
                "gev": None, "owr": [None, None], "comb": None, "gv": None}
        last_wr = None
        for grp in range(8):
            for fc in range(128):
                c, rem = divmod(fc, 16)
                H, q = divmod(rem, 4)
                iz = fc % 2
                zmm = k.emit("tensor", lambda e, c=c, fc=fc, grp=grp, iz=iz: e.matmul(
                    ps_z[:, iz * 256:(iz + 1) * 256],
                    BDM[:, fc * 128:(fc + 1) * 128],
                    xn_fm[:, c * T + grp * 256: c * T + (grp + 1) * 256],
                    start=True, stop=True),
                    deps=([asm[-1], xn_loads[-1], prev["zev"][iz]]
                          if (grp == 0 and fc < 2) else [prev["zev"][iz]]))
                hev = k.emit("scalar", lambda e, iz=iz: e.activation(
                    h_fc[:, iz * 256:(iz + 1) * 256], ps_z[:, iz * 256:(iz + 1) * 256],
                    AF.Gelu), deps=[zmm, prev["cpmm"][iz]] +
                    ([cp_writes[-1]] if (grp == 0 and fc < 2) else []))
                prev["zev"][iz] = hev
                wld = k.dma("sync", wcT[:, iz * 1024:(iz + 1) * 1024],
                            scr_cp[0:D, fc * 128:(fc + 1) * 128],
                            deps=[cp_writes[-1], prev["wld"][iz]]
                            if (grp == 0 and fc < 2) else [prev["wld"][iz]],
                            transpose=True)
                mm = None
                for tsub in range(2):
                    for half in range(2):
                        mm = k.emit("tensor",
                                    lambda e, fc=fc, iz=iz, tsub=tsub, half=half: e.matmul(
                                        ps_g[:, (tsub * 2 + half) * 512:
                                             (tsub * 2 + half + 1) * 512],
                                        h_fc[:, iz * 256 + tsub * 128:
                                             iz * 256 + tsub * 128 + 128],
                                        wcT[:, iz * 1024 + half * 512:
                                            iz * 1024 + (half + 1) * 512],
                                        start=(fc == 0), stop=(fc == 127)),
                                    deps=[hev, wld] +
                                    ([prev["gev"]] if (fc == 0 and tsub == 0
                                                       and half == 0) else []))
                prev["wld"][iz] = mm
                prev["cpmm"][iz] = mm
            if debug and grp == 0:
                k.dma("gpsimd", h_dbg, h_fc[:], deps=[prev["zev"][0], prev["zev"][1]])
            for tsub in range(2):
                tau = grp * 2 + tsub
                ldx = k.dma("gpsimd", x1_t[:], x1_hbm[RT(tau, 128)],
                            deps=[prev["comb"]])
                lds = k.dma("gpsimd", s1_t[:], s1_hbm[RT(tau, 128)], deps=[])
                gv = k.emit("scalar", lambda e, tsub=tsub: e.activation(
                    g_t[:, 0:512], ps_g[:, tsub * 1024: tsub * 1024 + 512],
                    AF.Silu), deps=[mm, prev["gv"]])
                gv2 = k.emit("scalar", lambda e, tsub=tsub: e.activation(
                    g_t[:, 512:1024], ps_g[:, tsub * 1024 + 512: (tsub + 1) * 1024],
                    AF.Silu), deps=[gv])
                prev["gv"] = gv2
                if tsub == 1:
                    prev["gev"] = gv2
                ot = o_t[:, (tau % 2) * D:(tau % 2 + 1) * D]
                sg = k.emit("vector", lambda e, ot=ot: e.tensor_tensor(
                    ot, s1_t[:], g_t[:], ALU.mult),
                    deps=[gv2, lds, prev["owr"][tau % 2]])
                x2v = k.emit("vector", lambda e, ot=ot: e.tensor_tensor(
                    ot, ot, x1_t[:], ALU.add), deps=[sg, ldx])
                prev["comb"] = x2v
                wr = k.dma("gpsimd", out_h[RT(tau, 128)], ot, deps=[x2v])
                prev["owr"][tau % 2] = wr
                last_wr = wr
        mn.close()
        pmn.close()
        k.emit("vector", lambda e: e.engine_nop(), deps=[last_wr])
        k.finalize()
    return k



def kernel(L0_attn_x, L0_attn_Wkqv, L0_attn_bkqv, L0_attn_Wp, L0_attn_bp,
           L0_ffn_w_up, L0_ffn_w_down, L1_attn_A_log, L1_attn_Dp,
           L1_attn_W_xproj, L1_attn_b_xproj, L1_attn_W_out, L1_attn_b_out,
           L1_ffn_rail_maker, L1_ffn_rail_up_proj, L1_ffn_w_cproj):
    C = lambda a: np.ascontiguousarray(a)
    ncores = 8
    xs = [C(L0_attn_x[i]) for i in range(ncores)]
    wkqv = C(L0_attn_Wkqv); bkqv = C(L0_attn_bkqv)
    wp = C(L0_attn_Wp); bp = C(L0_attn_bp)
    wup = C(L0_ffn_w_up); wdn = C(L0_ffn_w_down)
    alog = C(L1_attn_A_log); dp = C(L1_attn_Dp)
    wxp = C(L1_attn_W_xproj); bxp = C(L1_attn_b_xproj)
    wout = C(L1_attn_W_out); bout = C(L1_attn_b_out)
    mk = C(L1_ffn_rail_maker); up = C(L1_ffn_rail_up_proj)
    wcp = C(L1_ffn_w_cproj)

    ka = build_stage_a()
    ra = ka.run([{"x": xs[i], "Wkqv": wkqv, "bkqv": bkqv, "Wp": wp, "bp": bp}
                 for i in range(ncores)]).results

    kb = build_stage_b()
    rb = kb.run([{"x": xs[i], "xn_tm": ra[i]["xn_tm"], "s0_tm": ra[i]["s0_tm"],
                  "w_up": wup, "w_down": wdn} for i in range(ncores)]).results

    kc = build_stage_c()
    rc = kc.run([{"x1_tm": rb[i]["x1_tm"], "A_log": alog, "Dp": dp,
                  "W_xproj": wxp, "b_xproj": bxp, "W_out": wout, "b_out": bout}
                 for i in range(ncores)]).results

    kd = build_stage_d()
    rd = kd.run([{"x1_tm": rb[i]["x1_tm"], "xn1_tm": rc[i]["xn1_tm"],
                  "s1_tm": rc[i]["s1_tm"], "rail_maker": mk,
                  "rail_up_proj": up, "w_cproj": wcp}
                 for i in range(ncores)]).results

    return np.stack([rd[i]["out"] for i in range(ncores)], axis=0)


# version 03
# speedup vs baseline: 1.1605x; 1.1605x over previous
import contextlib
import numpy as np
import concourse.bass as bass
import concourse.mybir as mybir
from concourse.bass_utils import run_bass_kernel_spmd

F32 = mybir.dt.float32
BF16 = mybir.dt.bfloat16
AF = mybir.ActivationFunctionType
ALU = mybir.AluOpType
AX = mybir.AxisListType

T = 2048
D = 1024
NCH = 8
NTT = 16
NH = 16
EH = 2112
DFF = 4096
NST = 16
SHIFT = -8.0
RT = bass.ts


class _Op:
    __slots__ = ("eng", "kind", "fn", "deps", "need", "seq")

    def __init__(self, eng, kind, fn, deps):
        self.eng, self.kind, self.fn = eng, kind, fn
        self.deps = [d for d in deps if d is not None]
        self.need = False
        self.seq = None


class KB:
    ENGS = ("sync", "gpsimd", "vector", "scalar", "tensor")

    def __init__(self, name="prog"):
        self.nc = bass.Bass("TRN2", target_bir_lowering=False)
        self.es = contextlib.ExitStack()
        self.ops = {e: [] for e in self.ENGS}
        self.name = name
        self._uid = 0
        self._bar = None
        self._bar_seen = set()

    def _nm(self, name):
        self._uid += 1
        return f"{name}_{self._uid}"

    def sb(self, scope, name, shape, dtype):
        return scope.enter_context(self.nc.sbuf_tensor(self._nm(name), shape, dtype))

    def ps(self, scope, name, shape, dtype=F32):
        return scope.enter_context(self.nc.psum_tensor(self._nm(name), shape, dtype))

    def barrier(self):
        toks = []
        for eng in self.ENGS:
            if self.ops[eng]:
                toks.append(self.ops[eng][-1])
        self._bar = toks
        self._bar_seen = set()

    def dram(self, name, shape, dtype, kind):
        return self.nc.dram_tensor(name, shape, dtype, kind=kind).ap()

    def kcol(self, value):
        if not hasattr(self, "_kc"):
            self._kc = {}
            self._kc_sc = contextlib.ExitStack()
            self.es.enter_context(self._kc_sc)
        if value not in self._kc:
            t = self.sb(self._kc_sc, f"kc_{len(self._kc)}", [128, 1], F32)
            op = self.emit("gpsimd", lambda e, t=t, value=value: e.memset(t[:], value))
            self._kc[value] = (t[:], op)
        return self._kc[value]

    def emit(self, eng, fn, deps=(), dma=False):
        deps = list(deps)
        if self._bar is not None and eng not in self._bar_seen:
            deps = deps + self._bar
            self._bar_seen.add(eng)
            if len(self._bar_seen) == len(self.ENGS):
                self._bar = None
        op = _Op(eng, "d" if dma else "c", fn, deps)
        self.ops[eng].append(op)
        return op

    def dma(self, eng, out, in_, deps=(), **kw):
        return self.emit(eng, lambda e: e.dma_start(out=out, in_=in_, **kw),
                         deps, dma=True)

    @staticmethod
    def _implied(d, op):
        if d.eng != op.eng:
            return False
        if d.kind == "d" and op.kind == "d":
            return True
        return d.kind == "c" and op.kind == "c" and op.eng == "tensor"

    def finalize(self):
        for eng in self.ENGS:
            for op in self.ops[eng]:
                for d in op.deps:
                    if not self._implied(d, op):
                        d.need = True
        sems = {}
        for eng in self.ENGS:
            cc = cd = 0
            for op in self.ops[eng]:
                if op.kind == "d":
                    cd += 1
                    op.seq = cd * 16
                    op.need = True
                elif op.need:
                    cc += 1
                    op.seq = cc
            if cd:
                sems[(eng, "d")] = self.es.enter_context(
                    self.nc.semaphore(f"s_{eng}_d"))
            if cc:
                sems[(eng, "c")] = self.es.enter_context(
                    self.nc.semaphore(f"s_{eng}_c"))
        blk = self.es.enter_context(self.nc.Block())

        def emit_engine(e, eng):
            for op in self.ops[eng]:
                waits = {}
                for d in op.deps:
                    if self._implied(d, op):
                        continue
                    key = (d.eng, d.kind)
                    waits[key] = max(waits.get(key, 0), d.seq)
                for key, val in sorted(waits.items()):
                    e.wait_ge(sems[key], val)
                instr = op.fn(e)
                if op.need:
                    instr.then_inc(sems[(eng, op.kind)], 16 if op.kind == "d" else 1)

        for eng in self.ENGS:
            if self.ops[eng]:
                getattr(blk, eng)(lambda e, _eng=eng: emit_engine(e, _eng))

    def run(self, in_maps, core_ids=None, **kw):
        if core_ids is None:
            core_ids = list(range(len(in_maps)))
        return run_bass_kernel_spmd(self.nc, in_maps, core_ids=core_ids, **kw)



def emit_ln_tm(k, sc, x_hbm, xn_scr, pre=(), war=()):
    x_t = k.sb(sc, "ln_x", [128, 2 * D], F32)
    xn_t = k.sb(sc, "ln_xn", [128, 2 * D], BF16)
    sq_t = k.sb(sc, "ln_sq", [128, D], F32)
    cols = k.sb(sc, "ln_cols", [128, 16], F32)
    writes = []
    prev = {"cast": [None, None], "r2": None, "ln": None, "rstd_rd": [None, None],
            "wr": [None, None]}
    for i in range(NTT):
        b = i % 2
        co = (i % 2) * 8
        xt = x_t[:, b * D:(b + 1) * D]
        xnt = xn_t[:, b * D:(b + 1) * D]
        ldx = k.dma("sync", xt, x_hbm[RT(i, 128)],
                    deps=list(pre) + list(war) + [prev["cast"][b]])
        sq = k.emit("scalar", lambda e, xt=xt: e.activation(sq_t[:], xt, AF.Square),
                    deps=[ldx, prev["r2"]])
        r1 = k.emit("vector", lambda e, xt=xt, co=co: e.tensor_reduce(
            cols[:, co:co + 1], xt, AX.X, ALU.add), deps=[ldx])
        r2 = k.emit("vector", lambda e, co=co: e.tensor_reduce(
            cols[:, co + 1:co + 2], sq_t[:], AX.X, ALU.add), deps=[sq])
        prev["r2"] = r2
        m = k.emit("vector", lambda e, co=co: e.tensor_scalar(
            cols[:, co:co + 1], cols[:, co:co + 1], 1.0 / D, None, ALU.mult), deps=[r1])
        mm = k.emit("vector", lambda e, co=co: e.tensor_tensor(
            cols[:, co + 3:co + 4], cols[:, co:co + 1], cols[:, co:co + 1], ALU.mult),
            deps=[m])
        var = k.emit("vector", lambda e, co=co: e.tensor_scalar(
            cols[:, co + 2:co + 3], cols[:, co + 1:co + 2], 1.0 / D, None, ALU.mult),
            deps=[r2, prev["ln"]])
        var2 = k.emit("vector", lambda e, co=co: e.tensor_tensor(
            cols[:, co + 2:co + 3], cols[:, co + 2:co + 3], cols[:, co + 3:co + 4],
            ALU.subtract), deps=[mm])
        eps_ap, eps_op = k.kcol(1e-5)
        ln = k.emit("scalar", lambda e, co=co: e.activation(
            cols[:, co + 4:co + 5], cols[:, co + 2:co + 3], AF.Ln, bias=eps_ap),
            deps=[var2, eps_op])
        prev["ln"] = ln
        rstd = k.emit("scalar", lambda e, co=co: e.activation(
            cols[:, co + 5:co + 6], cols[:, co + 4:co + 5], AF.Exp, scale=-0.5),
            deps=[ln, prev["rstd_rd"][b]])
        cast = k.emit("vector", lambda e, xt=xt, xnt=xnt, co=co: e.tensor_scalar(
            xnt, xt, cols[:, co:co + 1], cols[:, co + 5:co + 6],
            ALU.subtract, ALU.mult), deps=[rstd, prev["wr"][b]])
        prev["rstd_rd"][b] = cast
        prev["cast"][b] = cast
        wr = k.dma("gpsimd", xn_scr[RT(i, 128)], xnt, deps=[cast])
        prev["wr"][b] = wr
        writes.append(wr)
    return writes, prev["cast"]


def emit_stage_a(k, io, debug=False):
    x_hbm = io['x']; wkqv = io['Wkqv']; bkqv = io['bkqv']
    wp_h = io['Wp']; bp_h = io['bp']
    xn_out = io['xn_tm']; s0_out = io['s0_tm']
    P = contextlib.ExitStack()
    xn_fm = k.sb(P, "xn_fm", [128, NCH * T], BF16)
    v_tm = k.sb(P, "v_tm", [128, NTT * D], BF16)
    sa_fm = k.sb(P, "sa_fm", [128, NCH * T], BF16)
    ones_row = k.sb(P, "ones_row", [1, 512], BF16)
    ones_rf = k.sb(P, "ones_rf", [1, 64], F32)
    ones_col = k.sb(P, "ones_col", [128, 1], BF16)
    masks = k.sb(P, "masks", [128, 4 * 512], BF16)
    msk_sc = contextlib.ExitStack()
    iot_f = k.sb(msk_sc, "iot_f", [128, 512], F32)
    iot_p = k.sb(msk_sc, "iot_p", [128, 1], F32)
    diff_t = k.sb(msk_sc, "diff_t", [128, 512], F32)

    c_if = k.emit("gpsimd", lambda e: e.iota(
        iot_f[:], [[1, 512]], channel_multiplier=0,
        allow_small_or_imprecise_dtypes=True))
    c_ip = k.emit("gpsimd", lambda e: e.iota(
        iot_p[:], [[1, 1]], channel_multiplier=1,
        allow_small_or_imprecise_dtypes=True))
    k.emit("gpsimd", lambda e: e.memset(ones_row[:], 1.0))
    c_orf = k.emit("gpsimd", lambda e: e.memset(ones_rf[:], 1.0))
    c_oc = k.emit("gpsimd", lambda e: e.memset(ones_col[:], 1.0))
    c_d = k.emit("vector", lambda e: e.tensor_scalar(
        diff_t[:], iot_f[:], iot_p[:], None, ALU.subtract), deps=[c_if, c_ip])
    mask_last = None
    for d in range(4):
        mask_last = k.emit("vector", lambda e, d=d: e.tensor_scalar(
            masks[:, d * 512:(d + 1) * 512], diff_t[:], float(128 * d),
            None, ALU.is_ge), deps=[c_d])

    msk_sc.close()
    ln_sc = contextlib.ExitStack()
    xn_writes, ln_last = emit_ln_tm(k, ln_sc, x_hbm, xn_out, war=[mask_last, c_d])
    ln_sc.close()

    xn_loads = []
    for c in range(NCH):
        xn_loads.append(k.dma("sync", xn_fm[:, c * T:(c + 1) * T],
                              xn_out[:, RT(c, 128)],
                              deps=xn_writes if c == 0 else [], transpose=True))

    wv_sc = contextlib.ExitStack()
    wv_f = k.sb(wv_sc, "wv_f", [128, NCH * 64], F32)
    wv_b = k.sb(wv_sc, "wv_b", [128, NCH * NH * 64], BF16)
    bv_f = k.sb(wv_sc, "bv_f", [1, NH * 64], F32)
    bv_b = k.sb(wv_sc, "bv_b", [1, NH * 64], BF16)
    ps_sc_ = contextlib.ExitStack()
    ps_a = k.ps(ps_sc_, "ps_a", [128, 2 * 512])
    ps_b2 = k.ps(ps_sc_, "ps_b2", [128, 2 * 512])
    ldbv = k.dma("gpsimd", bv_f[:].rearrange("o (h e) -> o h e", h=NH),
                 bkqv[:, 2048:2112].unsqueeze(0))
    cbv = k.emit("vector", lambda e: e.tensor_copy(bv_b[:], bv_f[:]), deps=[ldbv])
    wv_casts = []
    prev_ld = None
    for h in range(NH):
        src = wkqv[h].rearrange("(c p) e -> p c e", p=128)[:, :, 2048:2112]
        ld = k.dma("sync", wv_f[:].rearrange("p (c f) -> p c f", c=NCH), src,
                   deps=(list(ln_last) if h == 0 else [prev_ld]))
        ca = k.emit("vector", lambda e, h=h: e.tensor_copy(
            wv_b[:].rearrange("p (c g f) -> p c g f", c=NCH, g=NH)[:, :, h],
            wv_f[:].rearrange("p (c f) -> p c f", c=NCH)), deps=[ld])
        prev_ld = ca
        wv_casts.append(ca)
    v_evs = [None, None]
    for i in range(NTT):
        psv = [ps_a, ps_b2][i % 2]
        mm = None
        for half in range(2):
            for c in range(NCH):
                deps = []
                if half == 0 and c == 0:
                    deps = [wv_casts[-1], xn_loads[-1], v_evs[i % 2]]
                k.emit("tensor",
                       lambda e, i=i, c=c, half=half, psv=psv: e.matmul(
                           psv[:, half * 512:(half + 1) * 512],
                           xn_fm[:, c * T + i * 128: c * T + (i + 1) * 128],
                           wv_b[:, c * NH * 64 + half * 512:
                                c * NH * 64 + (half + 1) * 512],
                           start=(c == 0), stop=False), deps=deps)
            mm = k.emit("tensor", lambda e, half=half, psv=psv: e.matmul(
                psv[:, half * 512:(half + 1) * 512],
                ones_row[0:1, 0:128], bv_b[:, half * 512:(half + 1) * 512],
                start=False, stop=True), deps=[cbv] if half == 0 else [])
        ev = k.emit("scalar", lambda e, i=i, psv=psv: e.copy(
            v_tm[:, i * D:(i + 1) * D], psv[:, 0:1024]), deps=[mm])
        v_evs[i % 2] = ev

    wv_sc.close()
    at_sc = contextlib.ExitStack()
    k_fm = k.sb(at_sc, "k_fm", [128, NCH * T], BF16)
    q_fm = k.sb(at_sc, "q_fm", [128, NCH * T], BF16)
    wblk_f = k.sb(at_sc, "wblk_f", [128, 2 * 1024], F32)
    wblk_b = k.sb(at_sc, "wblk_b", [128, 2 * 1024], BF16)
    p_sb = k.sb(at_sc, "p_sb", [128, 2 * 512], BF16)
    dn_sb = k.sb(at_sc, "dn_sb", [1, T], F32)
    bc_sb = k.sb(at_sc, "bc_sb", [64, 2 * 512], F32)
    br_f = k.sb(at_sc, "br_f", [1, EH], F32)
    br_b = k.sb(at_sc, "br_b", [1, EH], BF16)
    ps_sa = k.ps(ps_sc_, "ps_sa", [65, T])

    prev = {"wcast": [None, None], "blkmm": [None, None],
            "kqev": [None, None], "exp": [None, None], "pml": [None, None],
            "bcc": [None, None], "nrm": [None, None], "brow": None}
    hd_done = None
    sa_last = None
    exp_any = [None]
    for h in range(NH):
        ldbr = k.dma("gpsimd", br_f[:], bkqv[h].unsqueeze(0),
                     deps=[prev["brow"]] if h else [v_evs[0], v_evs[1]])
        cbr = k.emit("vector", lambda e: e.tensor_copy(br_b[:], br_f[:]),
                     deps=[ldbr])
        kq_ev_last = None
        for bb in range(16):
            ib = (h * 16 + bb) % 2
            src = wkqv[h].rearrange("(c p) e -> p c e", p=128)[:, :, bb * 128:(bb + 1) * 128]
            dstf = wblk_f[:].rearrange("p (i c f) -> p i c f", i=2, c=NCH)[:, ib]
            ldw = k.dma("sync", dstf, src,
                        deps=[prev["wcast"][ib]] if prev["wcast"][ib]
                        else [v_evs[0], v_evs[1]])
            wcast = k.emit("vector", lambda e, ib=ib: e.tensor_copy(
                wblk_b[:, ib * 1024:(ib + 1) * 1024],
                wblk_f[:, ib * 1024:(ib + 1) * 1024]),
                deps=[ldw, prev["blkmm"][ib]])
            prev["wcast"][ib] = wcast
            for w in range(4):
                iw = (bb * 4 + w) % 2
                pskq = ps_a[:, iw * 512:(iw + 1) * 512]
                deps0 = [wcast, v_evs[0], v_evs[1], prev["kqev"][iw],
                         hd_done]
                mm = None
                for c in range(NCH):
                    mm = k.emit("tensor",
                                lambda e, c=c, w=w, pskq=pskq, ib=ib: e.matmul(
                                    pskq,
                                    wblk_b[:, ib * 1024 + c * 128:
                                           ib * 1024 + (c + 1) * 128],
                                    xn_fm[:, c * T + w * 512: c * T + (w + 1) * 512],
                                    start=(c == 0), stop=False),
                                deps=deps0 if c == 0 else [])
                mm = k.emit("tensor", lambda e, bb=bb, pskq=pskq: e.matmul(
                    pskq, br_b[:, bb * 128:(bb + 1) * 128], ones_row[:],
                    start=False, stop=True), deps=[cbr])
                prev["blkmm"][ib] = mm
                if bb < 8:
                    ev = k.emit("scalar", lambda e, bb=bb, w=w, pskq=pskq: e.copy(
                        k_fm[:, bb * T + w * 512: bb * T + (w + 1) * 512], pskq),
                        deps=[mm, sa_last])
                else:
                    ev = k.emit("scalar", lambda e, bb=bb, w=w, pskq=pskq: e.activation(
                        q_fm[:, (bb - 8) * T + w * 512: (bb - 8) * T + (w + 1) * 512],
                        pskq, AF.Copy, scale=1.0 / 32.0), deps=[mm, sa_last])
                prev["kqev"][iw] = ev
                kq_ev_last = ev
        nsc = 0
        pv_last = None
        dn_last = None
        for w in range(4):
            nkt = 4 * w + 4
            for kt in range(nkt):
                isc = nsc % 2
                pssc = ps_b2[:, isc * 512:(isc + 1) * 512]
                deps0 = [kq_ev_last, prev["exp"][isc], hd_done]
                mm = None
                for c in range(NCH):
                    mm = k.emit("tensor",
                                lambda e, c=c, kt=kt, w=w, pssc=pssc: e.matmul(
                                    pssc,
                                    k_fm[:, c * T + kt * 128: c * T + (kt + 1) * 128],
                                    q_fm[:, c * T + w * 512: c * T + (w + 1) * 512],
                                    start=(c == 0), stop=(c == NCH - 1)),
                                deps=deps0 if c == 0 else [])
                pdst = p_sb[:, isc * 512:(isc + 1) * 512]
                sh_ap, sh_op = k.kcol(SHIFT)
                ex = k.emit("scalar", lambda e, pssc=pssc, pdst=pdst, sh_ap=sh_ap:
                            e.activation(pdst, pssc, AF.Exp, bias=sh_ap),
                            deps=[mm, prev["pml"][isc], sh_op])
                prev["exp"][isc] = ex
                exp_any[0] = ex
                pm = ex
                if kt >= 4 * w:
                    dm = kt - 4 * w
                    pm = k.emit("vector", lambda e, pdst=pdst, dm=dm: e.tensor_tensor(
                        pdst, pdst, masks[:, dm * 512:(dm + 1) * 512], ALU.mult),
                        deps=[ex])
                pv_mm = k.emit("tensor", lambda e, kt=kt, w=w, pdst=pdst, h=h,
                               nkt=nkt: e.matmul(
                    ps_sa[0:64, w * 512:(w + 1) * 512],
                    v_tm[:, kt * D + h * 64: kt * D + h * 64 + 64], pdst,
                    start=(kt == 0), stop=(kt == nkt - 1)),
                    deps=[pm] + ([hd_done] if kt == 0 else []))
                dn_mm = k.emit("tensor", lambda e, kt=kt, w=w, pdst=pdst,
                               nkt=nkt: e.matmul(
                    ps_sa[64:65, w * 512:(w + 1) * 512], ones_col[:], pdst,
                    start=(kt == 0), stop=(kt == nkt - 1), skip_group_check=True),
                    deps=[c_oc] if (h == 0 and nsc == 0) else [])
                prev["pml"][isc] = dn_mm
                pv_last = pv_mm
                dn_last = dn_mm
                nsc += 1
        dcp = k.emit("scalar", lambda e: e.copy(dn_sb[:], ps_sa[64:65, :]),
                     deps=[dn_last, pv_last, hd_done])
        drc = k.emit("vector", lambda e: e.reciprocal(dn_sb[:], dn_sb[:]),
                     deps=[dcp])
        hd_ops = []
        for w in range(4):
            hw = w % 2
            psbc = ps_b2[0:64, hw * 512:(hw + 1) * 512]
            bcm = k.emit("tensor", lambda e, w=w, psbc=psbc: e.matmul(
                psbc, ones_rf[:], dn_sb[:, w * 512:(w + 1) * 512],
                start=True, stop=True),
                deps=[drc, prev["bcc"][hw], prev["exp"][hw], c_orf])
            bcc = k.emit("scalar", lambda e, w=w, psbc=psbc: e.copy(
                bc_sb[:, (w % 2) * 512:(w % 2 + 1) * 512], psbc),
                deps=[bcm, prev["nrm"][w % 2]])
            prev["exp"][hw] = bcc
            nrm = k.emit("vector", lambda e, w=w, h=h: e.tensor_tensor(
                sa_fm[64 * (h % 2):64 * (h % 2) + 64,
                      (h // 2) * T + w * 512:(h // 2) * T + (w + 1) * 512],
                ps_sa[0:64, w * 512:(w + 1) * 512],
                bc_sb[:, (w % 2) * 512:(w % 2 + 1) * 512], ALU.mult),
                deps=[bcc])
            prev["bcc"][w % 2] = bcc
            prev["nrm"][w % 2] = nrm
            hd_ops.append(nrm)
        hd_done = hd_ops[-1]
        prev["brow"] = cbr
        sa_last = hd_done

    if debug:
        d1 = k.dma("gpsimd", io['sa_dbg'], sa_fm[:], deps=[hd_done])
        d2 = k.dma("gpsimd", io['v_dbg'], v_tm[:], deps=[hd_done])
        k.emit("vector", lambda e: e.engine_nop(), deps=[d1, d2])
    at_sc.close()

    p5 = contextlib.ExitStack()
    wp_st = k.sb(p5, "wp_st", [128, D], F32)
    wp_b = k.sb(p5, "wp_b", [128, NCH * D], BF16)
    bp_f = k.sb(p5, "bp_f", [1, D], F32)
    bp_row = k.sb(p5, "bp_row", [1, D], BF16)
    s0_t = k.sb(p5, "s0_t", [128, 2 * D], F32)
    ldbp = k.dma("gpsimd", bp_f[:], bp_h.unsqueeze(0), deps=[hd_done])
    cbp = k.emit("vector", lambda e: e.tensor_copy(bp_row[:], bp_f[:]),
                 deps=[ldbp])
    prev_cast = None
    wp_casts = []
    for c in range(NCH):
        src = wp_h.rearrange("(c p) e -> c p e", p=128)[c]
        ld = k.dma("sync", wp_st[:], src,
                   deps=[prev_cast] if prev_cast else [hd_done, exp_any[0]])
        ca = k.emit("vector", lambda e, c=c: e.tensor_copy(
            wp_b[:, c * D:(c + 1) * D], wp_st[:]), deps=[ld])
        prev_cast = ca
        wp_casts.append(ca)
    prev_st = [None, None]
    final_sts = []
    for i in range(NTT):
        pss = [ps_a, ps_b2][i % 2]
        mm_f = None
        for c in range(NCH):
            deps0 = [wp_casts[-1], hd_done, prev["kqev"][0], prev["kqev"][1],
                     prev_st[i % 2]] if c == 0 else []
            k.emit("tensor", lambda e, i=i, c=c, pss=pss: e.matmul(
                pss[:, 0:512],
                sa_fm[:, c * T + i * 128: c * T + (i + 1) * 128],
                wp_b[:, c * D: c * D + 512],
                start=(c == 0), stop=False), deps=deps0)
            k.emit("tensor", lambda e, i=i, c=c, pss=pss: e.matmul(
                pss[:, 512:1024],
                sa_fm[:, c * T + i * 128: c * T + (i + 1) * 128],
                wp_b[:, c * D + 512: c * D + 1024],
                start=(c == 0), stop=False), deps=[])
        m1 = k.emit("tensor", lambda e, pss=pss: e.matmul(
            pss[:, 0:512], ones_row[0:1, 0:128], bp_row[:, 0:512],
            start=False, stop=True), deps=[cbp])
        m2 = k.emit("tensor", lambda e, pss=pss: e.matmul(
            pss[:, 512:1024], ones_row[0:1, 0:128], bp_row[:, 512:1024],
            start=False, stop=True), deps=[])
        ev = k.emit("scalar", lambda e, i=i, pss=pss: e.copy(
            s0_t[:, (i % 2) * D:(i % 2 + 1) * D], pss[:, 0:1024]),
            deps=[m1, m2])
        st = k.dma("gpsimd", s0_out[RT(i, 128)],
                   s0_t[:, (i % 2) * D:(i % 2 + 1) * D], deps=[ev])
        prev_st[i % 2] = st
        final_sts.append(st)
    p5.close()

    k.emit("vector", lambda e: e.engine_nop(),
           deps=[final_sts[-1], final_sts[-2], xn_writes[-1]])
    ps_sc_.close()
    P.close()


def build_stage_a(debug=False):
    k = KB("stage_a")
    io = {}
    io['x'] = k.dram("x", [T, D], F32, "ExternalInput")
    io['Wkqv'] = k.dram("Wkqv", [NH, D, EH], F32, "ExternalInput")
    io['bkqv'] = k.dram("bkqv", [NH, EH], F32, "ExternalInput")
    io['Wp'] = k.dram("Wp", [D, D], F32, "ExternalInput")
    io['bp'] = k.dram("bp", [D], F32, "ExternalInput")
    io['xn_tm'] = k.dram("xn_tm", [T, D], BF16, "ExternalOutput")
    io['s0_tm'] = k.dram("s0_tm", [T, D], F32, "ExternalOutput")
    if debug:
        io['sa_dbg'] = k.dram("sa_dbg", [128, NCH * T], BF16, "ExternalOutput")
        io['v_dbg'] = k.dram("v_dbg", [128, NTT * D], BF16, "ExternalOutput")
    with k.es:
        k.kcol(1e-5)
        k.kcol(SHIFT)
        emit_stage_a(k, io, debug=debug)
        k.finalize()
    return k



def emit_ternarize(k, w_hbm, wq_scr, rows, cols, wst, absb, accb, tcol,
                   ones_rf, ones_cf, ps_tmp, sgn_b, msk_b, out_b, pre=()):
    nt = rows // 128
    acc_ops = []
    prev_abs = [None, None]
    for i in range(nt):
        b = i % 2
        wt = wst[:, b * cols:(b + 1) * cols]
        ld = k.dma("sync", wt, w_hbm[RT(i, 128)],
                   deps=(list(pre) if i < 2 else []) + [prev_abs[b]])
        ab = k.emit("scalar", lambda e, wt=wt, i=i: e.activation(
            absb[:, 0:cols], wt, AF.Abs, accum_out=accb[:, i:i + 1]), deps=[ld])
        prev_abs[b] = ab
        acc_ops.append(ab)
    red = k.emit("vector", lambda e: e.tensor_reduce(
        accb[:, nt:nt + 1], accb[:, 0:nt], AX.X, ALU.add), deps=[acc_ops[-1]])
    smm = k.emit("tensor", lambda e: e.matmul(
        ps_tmp[0:1, 0:1], ones_cf[:], accb[:, nt:nt + 1], start=True, stop=True),
        deps=[red])
    thr = k.emit("scalar", lambda e: e.activation(
        tcol[0:1, 1:2], ps_tmp[0:1, 0:1], AF.Copy, scale=0.5 / (rows * cols)),
        deps=[smm])
    bcm = k.emit("tensor", lambda e: e.matmul(
        ps_tmp[:, 1:2], ones_rf[:], tcol[0:1, 1:2], start=True, stop=True),
        deps=[thr])
    bcp = k.emit("scalar", lambda e: e.copy(tcol[:, 0:1], ps_tmp[:, 1:2]),
                 deps=[bcm])
    prev_ld = [None, None]
    prev_cm = None
    prev_wq = None
    prev_wr = None
    writes = []
    for i in range(nt):
        b = i % 2
        wt = wst[:, b * cols:(b + 1) * cols]
        ld = k.dma("sync", wt, w_hbm[RT(i, 128)],
                   deps=[acc_ops[i], prev_ld[b]])
        sg = k.emit("scalar", lambda e, wt=wt: e.activation(
            sgn_b[:, 0:cols], wt, AF.Sign), deps=[ld, prev_wq])
        ab = k.emit("scalar", lambda e, wt=wt: e.activation(
            absb[:, 0:cols], wt, AF.Abs), deps=[ld, prev_cm])
        prev_ld[b] = ab
        cm = k.emit("vector", lambda e: e.tensor_scalar(
            msk_b[:, 0:cols], absb[:, 0:cols], tcol[:, 0:1], None, ALU.is_gt),
            deps=[ab, bcp, prev_wq])
        prev_cm = cm
        wq = k.emit("vector", lambda e: e.tensor_tensor(
            out_b[:, 0:cols], sgn_b[:, 0:cols], msk_b[:, 0:cols], ALU.mult),
            deps=[cm, sg, prev_wr])
        prev_wq = wq
        wr = k.dma("gpsimd", wq_scr[RT(i, 128)], out_b[:, 0:cols], deps=[wq])
        prev_wr = wr
        writes.append(wr)
    return writes


def emit_stage_b(k, io):
    x_hbm = io['x']; xn_hbm = io['xn_tm']; s0_hbm = io['s0_tm']
    wup_h = io['w_up']; wdn_h = io['w_down']
    wqu_scr = io['wqu']; wqd_scr = io['wqd']; x1_out = io['x1_tm']
    P = contextlib.ExitStack()
    xn_fm = k.sb(P, "xn_fm", [128, NCH * T], BF16)
    ones_col = k.sb(P, "ones_col", [128, 1], BF16)
    ones_cf = k.sb(P, "ones_cf", [128, 1], F32)
    ones_rf = k.sb(P, "ones_rf", [1, 128], F32)
    ident = k.sb(P, "ident", [128, 128], F32)
    rows_sb = k.sb(P, "rows_sb", [1, 8 * 512], F32)

    c_oc = k.emit("gpsimd", lambda e: e.memset(ones_col[:], 1.0))
    c_ocf = k.emit("gpsimd", lambda e: e.memset(ones_cf[:], 1.0))
    c_orf = k.emit("gpsimd", lambda e: e.memset(ones_rf[:], 1.0))
    mi = contextlib.ExitStack()
    iot_f = k.sb(mi, "iot_f", [128, 128], F32)
    iot_p = k.sb(mi, "iot_p", [128, 1], F32)
    c_if = k.emit("gpsimd", lambda e: e.iota(
        iot_f[:], [[1, 128]], channel_multiplier=0,
        allow_small_or_imprecise_dtypes=True))
    c_ip = k.emit("gpsimd", lambda e: e.iota(
        iot_p[:], [[1, 1]], channel_multiplier=1,
        allow_small_or_imprecise_dtypes=True))
    c_id = k.emit("vector", lambda e: e.tensor_scalar(
        ident[:], iot_f[:], iot_p[:], None, ALU.is_equal), deps=[c_if, c_ip])
    mi.close()

    xn_loads = []
    for c in range(NCH):
        xn_loads.append(k.dma("sync", xn_fm[:, c * T:(c + 1) * T],
                              xn_hbm[:, RT(c, 128)],
                              deps=[c_id] if c == 0 else [], transpose=True))

    tp = contextlib.ExitStack()
    wst = k.sb(tp, "wst", [128, 2 * DFF], F32)
    absb = k.sb(tp, "absb", [128, DFF], F32)
    accb = k.sb(tp, "accb", [128, 40], F32)
    tcol_u = k.sb(tp, "tcol_u", [128, 2], F32)
    tcol_d = k.sb(tp, "tcol_d", [128, 2], F32)
    sgn_b = k.sb(tp, "sgn_b", [128, DFF], BF16)
    msk_b = k.sb(tp, "msk_b", [128, DFF], BF16)
    out_b = k.sb(tp, "out_b", [128, DFF], BF16)
    pst = contextlib.ExitStack()
    ps_tmp = k.ps(pst, "ps_tmp", [128, 2])
    wr_up = emit_ternarize(k, wup_h, wqu_scr, DFF, D, wst, absb, accb,
                           tcol_u, ones_rf, ones_cf, ps_tmp, sgn_b, msk_b,
                           out_b, pre=[c_ocf, c_orf, c_id])
    wr_dn = emit_ternarize(k, wdn_h, wqd_scr, D, DFF, wst, absb, accb,
                           tcol_d, ones_rf, ones_cf, ps_tmp, sgn_b, msk_b,
                           out_b, pre=wr_up[-2:])
    tp.close()
    pst.close()

    wu = contextlib.ExitStack()
    wqT_up = k.sb(wu, "wqT_up", [128, NCH * DFF], BF16)
    wqT_loads = []
    for c in range(NCH):
        for fw in range(DFF // 512):
            wqT_loads.append(k.dma(
                "sync", wqT_up[:, c * DFF + fw * 512: c * DFF + (fw + 1) * 512],
                wqu_scr[RT(fw, 512), RT(c, 128)],
                deps=(wr_up + wr_dn[-1:]) if (c == 0 and fw == 0) else [],
                transpose=True))

    fb = contextlib.ExitStack()
    xr_fm = k.sb(fb, "xr_fm", [128, NCH * 512], BF16)
    h1 = k.sb(fb, "h1", [128, 32 * 512], BF16)
    sqb = k.sb(fb, "sqb", [128, 512], BF16)
    wdT = k.sb(fb, "wdT", [128, 2 * 1024], BF16)
    g_t = k.sb(fb, "g_t", [128, D], F32)
    x_t = k.sb(fb, "x_t", [128, D], F32)
    s0_t = k.sb(fb, "s0_t", [128, D], F32)
    sg_t = k.sb(fb, "sg_t", [128, D], F32)
    r2c = k.sb(fb, "r2c", [128, 1], F32)
    psf = contextlib.ExitStack()
    ps_1 = k.ps(psf, "ps_1", [128, 512])
    ps_2 = k.ps(psf, "ps_2", [128, 2 * 512])
    ps_r = k.ps(psf, "ps_r", [1, 512])
    ps_g = k.ps(psf, "ps_g", [128, 1024])
    ps_c = k.ps(psf, "ps_c", [128, 2])

    e8 = 1e-8
    prev = {"sq": None, "h1ev": [None, None], "rowrd": [None],
            "wdmm": [None, None], "gev": None, "x1wr": [None, None],
            "r2c": None, "comb": None, "xrrd": None, "bcmrd": None}
    last_x1_wr = None
    for w in range(4):
        ro = 0
        smm = None
        for c in range(NCH):
            sq = k.emit("vector", lambda e, c=c, w=w: e.tensor_tensor(
                sqb[:], xn_fm[:, c * T + w * 512: c * T + (w + 1) * 512],
                xn_fm[:, c * T + w * 512: c * T + (w + 1) * 512], ALU.mult),
                deps=[xn_loads[-1], prev["sq"]])
            smm = k.emit("tensor", lambda e, c=c, ro=ro: e.matmul(
                ps_r[0:1, ro:ro + 512], ones_col[:], sqb[:],
                start=(c == 0), stop=(c == NCH - 1)),
                deps=[sq, c_oc] + ([prev["rowrd"][0]] if c == 0 else []))
            prev["sq"] = smm
        r_sl = rows_sb[:, w * 512: w * 512 + 512]
        cp1 = k.emit("scalar", lambda e, ro=ro, r_sl=r_sl: e.activation(
            r_sl, ps_r[0:1, ro:ro + 512], AF.Ln), deps=[smm])
        cp2 = k.emit("scalar", lambda e, r_sl=r_sl: e.activation(
            r_sl, r_sl, AF.Exp, scale=0.5), deps=[cp1])
        cp3 = k.emit("vector", lambda e, r_sl=r_sl: e.tensor_scalar(
            r_sl, r_sl, 1.0 / 32.0, e8, ALU.mult, ALU.add), deps=[cp2])
        cp4 = k.emit("vector", lambda e, r_sl=r_sl: e.reciprocal(
            r_sl, r_sl), deps=[cp3])
        bcm = k.emit("tensor", lambda e, r_sl=r_sl: e.matmul(
            ps_1[:], ones_rf[:], r_sl,
            start=True, stop=True), deps=[cp4, c_orf, prev["bcmrd"]])
        xr_ops = []
        for c in range(NCH):
            xr = k.emit("vector", lambda e, c=c, w=w, ro=ro: e.tensor_tensor(
                xr_fm[:, c * 512:(c + 1) * 512],
                xn_fm[:, c * T + w * 512: c * T + (w + 1) * 512],
                ps_1[:], ALU.mult),
                deps=[bcm] + ([prev["h1ev"][0], prev["h1ev"][1]] if c == 0 else []))
            xr_ops.append(xr)
        prev["bcmrd"] = xr_ops[-1]
        s1_evs = []
        for ft in range(32):
            ip = ft % 2
            ps = ps_2[:, ip * 512:(ip + 1) * 512]
            mm = None
            for c in range(NCH):
                mm = k.emit("tensor", lambda e, c=c, ft=ft, ps=ps: e.matmul(
                    ps, wqT_up[:, c * DFF + ft * 128: c * DFF + (ft + 1) * 128],
                    xr_fm[:, c * 512:(c + 1) * 512],
                    start=(c == 0), stop=(c == NCH - 1)),
                    deps=([wqT_loads[-1], xr_ops[-1], prev["h1ev"][ip]]
                          if c == 0 else []))
            ev = k.emit("scalar", lambda e, ft=ft, ps=ps: e.activation(
                h1[:, ft * 512:(ft + 1) * 512], ps, AF.Gelu),
                deps=[mm, prev["gev"]])
            prev["h1ev"][ip] = ev
            s1_evs.append(ev)
        smm2 = None
        for ft in range(32):
            sq = k.emit("vector", lambda e, ft=ft: e.tensor_tensor(
                sqb[:], h1[:, ft * 512:(ft + 1) * 512],
                h1[:, ft * 512:(ft + 1) * 512], ALU.mult),
                deps=[s1_evs[ft], prev["sq"]])
            smm2 = k.emit("tensor", lambda e, ft=ft, ro=ro: e.matmul(
                ps_r[0:1, ro:ro + 512], ones_col[:], sqb[:],
                start=(ft == 0), stop=(ft == 31)),
                deps=[sq] + ([cp1] if ft == 0 else []))
            prev["sq"] = smm2
        r2_sl = rows_sb[:, 2048 + w * 512: 2048 + (w + 1) * 512]
        cq1 = k.emit("scalar", lambda e, r2_sl=r2_sl, ro=ro: e.activation(
            r2_sl, ps_r[0:1, ro:ro + 512], AF.Ln), deps=[smm2])
        cq2 = k.emit("scalar", lambda e, r2_sl=r2_sl: e.activation(
            r2_sl, r2_sl, AF.Exp, scale=0.5), deps=[cq1])
        cq3 = k.emit("vector", lambda e, r2_sl=r2_sl: e.tensor_scalar(
            r2_sl, r2_sl, 1.0 / 64.0, e8, ALU.mult, ALU.add), deps=[cq2])
        cq4 = k.emit("vector", lambda e, r2_sl=r2_sl: e.reciprocal(
            r2_sl, r2_sl), deps=[cq3])
        prev["rowrd"][0] = cq1
        for tt in range(4):
            tau = w * 4 + tt
            trm = k.emit("tensor", lambda e, r2_sl=r2_sl, tt=tt: e.transpose(
                ps_c[:, 0:1], r2_sl[:, tt * 128:(tt + 1) * 128], ident[0:1, 0:1]),
                deps=[cq4, c_id, prev["r2c"]])
            rcc = k.emit("scalar", lambda e: e.copy(r2c[:], ps_c[:, 0:1]),
                         deps=[trm])
            prev["r2c"] = rcc
            ipg = 0
            mm = None
            for ft in range(32):
                ld = k.dma("sync", wdT[:, (ft % 2) * 1024:(ft % 2 + 1) * 1024],
                           wqd_scr[0:D, RT(ft, 128)],
                           deps=[prev["wdmm"][ft % 2]], transpose=True)
                for half in range(2):
                    mm = k.emit("tensor",
                                lambda e, ft=ft, tt=tt, half=half, ipg=ipg: e.matmul(
                                    ps_g[:, ipg * 1024 + half * 512:
                                         ipg * 1024 + (half + 1) * 512],
                                    h1[:, ft * 512 + tt * 128: ft * 512 + tt * 128 + 128],
                                    wdT[:, (ft % 2) * 1024 + half * 512:
                                        (ft % 2) * 1024 + (half + 1) * 512],
                                    start=(ft == 0), stop=(ft == 31)),
                                deps=[ld, s1_evs[-1]] +
                                ([prev["gev"]] if (ft == 0 and half == 0) else []))
                prev["wdmm"][ft % 2] = mm
            ldx = k.dma("gpsimd", x_t[:], x_hbm[RT(tau, 128)],
                        deps=[prev["x1wr"][tau % 2]])
            lds = k.dma("gpsimd", s0_t[:], s0_hbm[RT(tau, 128)],
                        deps=[])
            gv = k.emit("scalar", lambda e, ipg=ipg: e.activation(
                g_t[:, 0:512], ps_g[:, ipg * 1024:ipg * 1024 + 512],
                AF.Silu, scale=r2c[:]), deps=[mm, rcc, prev["comb"]])
            gv2 = k.emit("scalar", lambda e, ipg=ipg: e.activation(
                g_t[:, 512:1024], ps_g[:, ipg * 1024 + 512:ipg * 1024 + 1024],
                AF.Silu, scale=r2c[:]), deps=[gv])
            prev["gev"] = gv2
            sg = k.emit("vector", lambda e: e.tensor_tensor(
                sg_t[:], s0_t[:], g_t[:], ALU.mult), deps=[gv2, lds])
            x1v = k.emit("vector", lambda e: e.tensor_tensor(
                sg_t[:], sg_t[:], x_t[:], ALU.add), deps=[sg, ldx])
            prev["comb"] = x1v
            wrx = k.dma("gpsimd", x1_out[RT(tau, 128)], sg_t[:], deps=[x1v])
            prev["x1wr"][tau % 2] = wrx
            last_x1_wr = wrx

    fb.close()
    psf.close()
    wu.close()
    k.emit("vector", lambda e: e.engine_nop(), deps=[last_x1_wr])
    P.close()


def build_stage_b():
    k = KB("stage_b")
    io = {}
    io['x'] = k.dram("x", [T, D], F32, "ExternalInput")
    io['xn_tm'] = k.dram("xn_tm", [T, D], BF16, "ExternalInput")
    io['s0_tm'] = k.dram("s0_tm", [T, D], F32, "ExternalInput")
    io['w_up'] = k.dram("w_up", [DFF, D], F32, "ExternalInput")
    io['w_down'] = k.dram("w_down", [D, DFF], F32, "ExternalInput")
    io['wqu'] = k.dram("wqu", [DFF, D], BF16, "Internal")
    io['wqd'] = k.dram("wqd", [D, DFF], BF16, "Internal")
    io['x1_tm'] = k.dram("x1_tm", [T, D], F32, "ExternalOutput")
    with k.es:
        k.kcol(1e-5)
        emit_stage_b(k, io)
        k.finalize()
    return k


def emit_stage_c(k, io):
    EHX = D + 2 * NST
    x1_hbm = io['x1_tm']; alog_h = io['A_log']; dp_h = io['Dp']
    wxp_h = io['W_xproj']; bxp_h = io['b_xproj']
    wout_h = io['W_out']; bout_h = io['b_out']
    scr_B = io['scrB']; scr_C = io['scrC']
    xn1_out = io['xn1_tm']; s1_out = io['s1_tm']
    P = contextlib.ExitStack()
    one_ap, one_op = k.kcol(1.0)
    xn_fm = k.sb(P, "xn_fm", [128, NCH * T], BF16)
    ones_row = k.sb(P, "ones_row", [1, 512], BF16)
    ones_rf = k.sb(P, "ones_rf", [1, 128], F32)
    ident1 = k.sb(P, "ident1", [1, 1], F32)
    ident16 = k.sb(P, "ident16", [16, 16], BF16)
    c_or = k.emit("gpsimd", lambda e: e.memset(ones_row[:], 1.0))
    c_orf = k.emit("gpsimd", lambda e: e.memset(ones_rf[:], 1.0))
    c_i1 = k.emit("gpsimd", lambda e: e.memset(ident1[:], 1.0))
    mi = contextlib.ExitStack()
    iot_f = k.sb(mi, "iot_f", [16, 16], F32)
    iot_p = k.sb(mi, "iot_p", [16, 1], F32)
    c_if = k.emit("gpsimd", lambda e: e.iota(
        iot_f[:], [[1, 16]], channel_multiplier=0,
        allow_small_or_imprecise_dtypes=True))
    c_ip = k.emit("gpsimd", lambda e: e.iota(
        iot_p[:], [[1, 1]], channel_multiplier=1,
        allow_small_or_imprecise_dtypes=True))
    c_id16 = k.emit("vector", lambda e: e.tensor_scalar(
        ident16[:], iot_f[:], iot_p[:], None, ALU.is_equal), deps=[c_if, c_ip])
    mi.close()

    ln_sc = contextlib.ExitStack()
    xn_writes, ln_last = emit_ln_tm(k, ln_sc, x1_hbm, xn1_out,
                                    war=[c_id16, c_i1])
    ln_sc.close()
    xn_loads = []
    for c in range(NCH):
        xn_loads.append(k.dma("sync", xn_fm[:, c * T:(c + 1) * T],
                              xn1_out[:, RT(c, 128)],
                              deps=xn_writes if c == 0 else [], transpose=True))

    wsc = contextlib.ExitStack()
    wxp_b = k.sb(wsc, "wxp_b", [128, NCH * EHX], BF16)
    wout_b = k.sb(wsc, "wout_b", [128, NCH * D], BF16)
    A_sb = k.sb(wsc, "A_sb", [128, 128], F32)
    dp_col = k.sb(wsc, "dp_col", [128, NCH], F32)
    bxp_b = k.sb(wsc, "bxp_b", [1, EHX], BF16)
    bout_b = k.sb(wsc, "bout_b", [1, D], BF16)
    stg = contextlib.ExitStack()
    dp_row = k.sb(stg, "dp_row", [1, D], F32)
    bxp_f = k.sb(stg, "bxp_f", [1, EHX], F32)
    bout_f = k.sb(stg, "bout_f", [1, D], F32)
    wstg = k.sb(stg, "wstg", [128, EHX], F32)
    pw = contextlib.ExitStack()
    ps_w = k.ps(pw, "ps_w", [128, 2])

    prev_c = None
    wx_casts = []
    for c in range(NCH):
        ld = k.dma("sync", wstg[:], wxp_h[RT(c, 128)],
                   deps=(list(ln_last) if c == 0 else [prev_c]))
        ca = k.emit("vector", lambda e, c=c: e.tensor_copy(
            wxp_b[:, c * EHX:(c + 1) * EHX], wstg[:]), deps=[ld])
        prev_c = ca
        wx_casts.append(ca)
    for c in range(NCH):
        ld = k.dma("sync", wstg[:, 0:D], wout_h[RT(c, 128)], deps=[prev_c])
        ca = k.emit("vector", lambda e, c=c: e.tensor_copy(
            wout_b[:, c * D:(c + 1) * D], wstg[:, 0:D]), deps=[ld])
        prev_c = ca
    wcast_last = prev_c
    lda = k.dma("sync", A_sb[:].rearrange("p (c n) -> p c n", c=NCH),
                alog_h.rearrange("(c p) n -> p c n", p=128), deps=[])
    a_exp = k.emit("scalar", lambda e: e.activation(A_sb[:], A_sb[:], AF.Exp),
                   deps=[lda])
    a_neg = k.emit("vector", lambda e: e.tensor_scalar(
        A_sb[:], A_sb[:], -1.0, None, ALU.mult), deps=[a_exp])
    ldp = k.dma("sync", dp_row[:], dp_h.unsqueeze(0), deps=[])
    dp_ops = []
    for c in range(NCH):
        tr = k.emit("tensor", lambda e, c=c: e.transpose(
            ps_w[:, 0:1], dp_row[0:1, c * 128:(c + 1) * 128], ident1[:]),
            deps=[ldp, c_i1] + dp_ops[-1:])
        cp = k.emit("scalar", lambda e, c=c: e.copy(dp_col[:, c:c + 1],
                                                    ps_w[:, 0:1]), deps=[tr])
        dp_ops.append(cp)
    ldbx = k.dma("sync", bxp_f[:], bxp_h.unsqueeze(0), deps=[])
    cbx = k.emit("vector", lambda e: e.tensor_copy(bxp_b[:], bxp_f[:]),
                 deps=[ldbx])
    ldbo = k.dma("sync", bout_f[:], bout_h.unsqueeze(0), deps=[])
    cbo = k.emit("vector", lambda e: e.tensor_copy(bout_b[:], bout_f[:]),
                 deps=[ldbo])

    stg.close()
    mb = contextlib.ExitStack()
    delta_w = k.sb(mb, "delta_w", [128, NCH * 512], F32)
    u_w = k.sb(mb, "u_w", [128, NCH * 512], F32)
    t1 = k.sb(mb, "t1", [128, 512], F32)
    B_nat = k.sb(mb, "B_nat", [16, 512], F32)
    C_tmb = k.sb(mb, "C_tmb", [128, 4 * 16], BF16)
    C_f16 = k.sb(mb, "C_f16", [16, 512], BF16)
    H_c = k.sb(mb, "H_c", [128, 512 * NST], F32)
    hc_b = k.sb(mb, "hc_b", [128, 512 * NST], BF16)
    C_rep = k.sb(mb, "C_rep", [128, 512 * NST], BF16)
    B_bc = k.sb(mb, "B_bc", [128, 2 * 512], F32)
    ab_b = k.sb(mb, "ab_b", [128, 2 * 512], F32)
    bx_b = k.sb(mb, "bx_b", [128, 2 * 512], F32)
    y_w = k.sb(mb, "y_w", [128, NCH * 512], BF16)
    state = k.sb(mb, "state", [128, NCH * NST], F32)
    s1_t = k.sb(mb, "s1_t", [128, 2 * D], F32)
    pm = contextlib.ExitStack()
    ps_p = k.ps(pm, "ps_p", [128, 2 * 512])
    ps_bc1 = k.ps(pm, "ps_bc1", [16, 512])
    ps_bc2 = k.ps(pm, "ps_bc2", [16, 512])
    ps_ct = k.ps(pm, "ps_ct", [128, 16], BF16)
    ps_s = k.ps(pm, "ps_s", [128, 1024])

    st0 = k.emit("gpsimd", lambda e: e.memset(state[:], 0.0),
                 deps=[cbx, cbo, dp_ops[-1], wcast_last, a_neg])
    state_ops = [st0] * NCH

    prev = {"dl": [None, None], "t1": None, "Bwr": None, "Cwr": None,
            "Bev": None, "Cev": [None], "Cf": None,
            "ab": [None, None], "bx": [None, None],
            "Bld": [None, None], "scan": [None, None], "hc": None,
            "red": None, "yst": None, "s1wr": [None, None], "s1ev": None,
            "crep": None, "proj": None, "uw": None, "hcrd": None}
    last_s1_wr = None
    for w in range(4):
        sp_ops = []
        for mt in range(NCH):
            ip = mt % 2
            ps = ps_p[:, ip * 512:(ip + 1) * 512]
            mm = None
            for c in range(NCH):
                mm = k.emit("tensor", lambda e, c=c, mt=mt, w=w, ps=ps: e.matmul(
                    ps, wxp_b[:, c * EHX + mt * 128: c * EHX + (mt + 1) * 128],
                    xn_fm[:, c * T + w * 512: c * T + (w + 1) * 512],
                    start=(c == 0), stop=False),
                    deps=([wcast_last, xn_loads[-1], prev["dl"][ip]]
                          if c == 0 else []))
            mm = k.emit("tensor", lambda e, mt=mt, ps=ps: e.matmul(
                ps, bxp_b[:, mt * 128:(mt + 1) * 128], ones_row[:],
                start=False, stop=True), deps=[cbx, c_or])
            ab1 = k.emit("scalar", lambda e, ps=ps: e.activation(
                t1[:], ps, AF.Abs), deps=[mm, prev["t1"]])
            ex1 = k.emit("scalar", lambda e: e.activation(
                t1[:], t1[:], AF.Exp, scale=-1.0), deps=[ab1])
            ln1 = k.emit("scalar", lambda e: e.activation(
                t1[:], t1[:], AF.Ln, bias=one_ap), deps=[ex1, one_op])
            dl = k.emit("vector", lambda e, mt=mt, ps=ps: e.scalar_tensor_tensor(
                delta_w[:, mt * 512:(mt + 1) * 512], ps, 0.0, t1[:],
                ALU.max, ALU.add), deps=[ln1])
            prev["t1"] = dl
            prev["dl"][ip] = dl
            sp_ops.append(dl)
        mmB = None
        mmC = None
        for c in range(NCH):
            mmB = k.emit("tensor", lambda e, c=c, w=w: e.matmul(
                ps_bc1[:], wxp_b[:, c * EHX + D: c * EHX + D + 16],
                xn_fm[:, c * T + w * 512: c * T + (w + 1) * 512],
                start=(c == 0), stop=False), deps=[prev["Bev"]] if c == 0 else [])
            mmC = k.emit("tensor", lambda e, c=c, w=w: e.matmul(
                ps_bc2[:], wxp_b[:, c * EHX + D + 16: c * EHX + D + 32],
                xn_fm[:, c * T + w * 512: c * T + (w + 1) * 512],
                start=(c == 0), stop=False), deps=[prev["Cev"][0]] if c == 0 else [])
        mmB = k.emit("tensor", lambda e: e.matmul(
            ps_bc1[:], bxp_b[:, D:D + 16], ones_row[:],
            start=False, stop=True), deps=[])
        mmC = k.emit("tensor", lambda e: e.matmul(
            ps_bc2[:], bxp_b[:, D + 16:D + 32], ones_row[:],
            start=False, stop=True), deps=[])
        bev = k.emit("vector", lambda e: e.tensor_copy(B_nat[:], ps_bc1[:]),
                     deps=[mmB, prev["Bwr"]])
        prev["Bev"] = bev
        bwr = k.dma("gpsimd", scr_B[:, w * 512:(w + 1) * 512], B_nat[:],
                    deps=[bev])
        prev["Bwr"] = bwr
        cf16 = k.emit("scalar", lambda e: e.copy(C_f16[:], ps_bc2[:]),
                      deps=[mmC, prev["Cf"]])
        cev_ops = []
        for tt in range(4):
            trc = k.emit("tensor", lambda e, tt=tt: e.transpose(
                ps_ct[:], C_f16[:, tt * 128:(tt + 1) * 128], ident16[:]),
                deps=[cf16, prev["Cwr"], c_id16] + cev_ops[-1:])
            cev = k.emit("scalar", lambda e, tt=tt: e.copy(
                C_tmb[:, tt * 16:(tt + 1) * 16], ps_ct[:]), deps=[trc])
            cev_ops.append(cev)
        prev["Cev"][0] = cf16
        prev["Cf"] = cev_ops[-1]
        cwr = k.dma("gpsimd",
                    scr_C[w * 512:(w + 1) * 512, :].rearrange(
                        "(tt p) n -> p tt n", p=128),
                    C_tmb[:].rearrange("p (tt n) -> p tt n", tt=4),
                    deps=cev_ops)
        prev["Cwr"] = cwr
        crep = k.dma("gpsimd", C_rep[:],
                     scr_C[w * 512:(w + 1) * 512, :].rearrange(
                         "t n -> (t n)").unsqueeze(0).partition_broadcast(
                         128).squeeze(1),
                     deps=[cwr, prev["crep"]])
        u_ops = []
        for c in range(NCH):
            uo = k.emit("vector", lambda e, c=c, w=w: e.tensor_tensor(
                u_w[:, c * 512:(c + 1) * 512],
                delta_w[:, c * 512:(c + 1) * 512],
                xn_fm[:, c * T + w * 512: c * T + (w + 1) * 512], ALU.mult),
                deps=[sp_ops[c], prev["uw"]])
            u_ops.append(uo)
        for c in range(NCH):
            for n in range(NST):
                ib = n % 2
                bld = k.dma("gpsimd", B_bc[:, ib * 512:(ib + 1) * 512],
                            scr_B[n, w * 512:(w + 1) * 512].unsqueeze(
                                0).partition_broadcast(128).squeeze(1),
                            deps=[bwr, prev["bx"][ib]] if (c == 0 and n < 2)
                            else [prev["bx"][ib]])
                abv = k.emit("scalar", lambda e, c=c, n=n, ib=ib: e.activation(
                    ab_b[:, ib * 512:(ib + 1) * 512],
                    delta_w[:, c * 512:(c + 1) * 512], AF.Exp,
                    scale=A_sb[:, c * 16 + n: c * 16 + n + 1]),
                    deps=[sp_ops[c], a_neg, prev["ab"][ib]])
                bxv = k.emit("vector", lambda e, c=c, ib=ib: e.tensor_tensor(
                    bx_b[:, ib * 512:(ib + 1) * 512],
                    u_w[:, c * 512:(c + 1) * 512],
                    B_bc[:, ib * 512:(ib + 1) * 512], ALU.mult),
                    deps=[u_ops[c], bld, prev["scan"][ib]])
                hsl = H_c[:].rearrange("p (t n) -> p t n", n=NST)[:, :, n]
                sc = k.emit("vector",
                            lambda e, c=c, n=n, ib=ib, hsl=hsl: e.tensor_tensor_scan(
                                hsl, ab_b[:, ib * 512:(ib + 1) * 512],
                                bx_b[:, ib * 512:(ib + 1) * 512],
                                state[:, c * NST + n: c * NST + n + 1],
                                ALU.mult, ALU.add),
                            deps=[abv, bxv, state_ops[c]] +
                            ([prev["hcrd"]] if n == 0 else []))
                prev["ab"][ib] = sc
                prev["bx"][ib] = sc
                prev["scan"][ib] = sc
            sts = k.emit("vector", lambda e, c=c: e.tensor_copy(
                state[:, c * NST:(c + 1) * NST],
                H_c[:, 511 * NST:512 * NST]), deps=[prev["scan"][0],
                                                    prev["scan"][1]])
            state_ops[c] = sts
            hc = k.emit("vector", lambda e: e.tensor_tensor(
                hc_b[:], H_c[:], C_rep[:], ALU.mult),
                deps=[sts, crep, prev["red"]])
            prev["hc"] = hc
            red = k.emit("vector", lambda e: e.tensor_reduce(
                t1[:], hc_b[:].rearrange("p (t n) -> p t n", n=NST),
                AX.X, ALU.add), deps=[hc, prev["yst"]])
            prev["red"] = red
            yst = k.emit("vector", lambda e, c=c, w=w: e.scalar_tensor_tensor(
                y_w[:, c * 512:(c + 1) * 512],
                xn_fm[:, c * T + w * 512: c * T + (w + 1) * 512],
                dp_col[:, c:c + 1], t1[:], ALU.mult, ALU.add),
                deps=[red, dp_ops[-1], prev["s1ev"]])
            prev["yst"] = yst
            prev["hcrd"] = yst
        prev["uw"] = prev["hcrd"]
        prev["t1"] = prev["yst"]
        prev["crep"] = prev["hc"]
        for tt in range(4):
            tau = w * 4 + tt
            mms = []
            for c in range(NCH):
                for half in range(2):
                    mm = k.emit("tensor",
                                lambda e, c=c, tt=tt, half=half: e.matmul(
                                    ps_s[:, half * 512:(half + 1) * 512],
                                    y_w[:, c * 512 + tt * 128: c * 512 + tt * 128 + 128],
                                    wout_b[:, c * D + half * 512:
                                           c * D + (half + 1) * 512],
                                    start=(c == 0), stop=False),
                                deps=([prev["yst"], prev["s1ev"]]
                                      if (c == 0 and half == 0) else []))
            m1 = k.emit("tensor", lambda e: e.matmul(
                ps_s[:, 0:512], ones_row[0:1, 0:128], bout_b[:, 0:512],
                start=False, stop=True), deps=[cbo])
            m2 = k.emit("tensor", lambda e: e.matmul(
                ps_s[:, 512:1024], ones_row[0:1, 0:128], bout_b[:, 512:1024],
                start=False, stop=True), deps=[])
            ev = k.emit("scalar", lambda e, tt=tt: e.copy(
                s1_t[:, (tt % 2) * D:(tt % 2 + 1) * D], ps_s[:]),
                deps=[m1, m2, prev["s1wr"][tt % 2]])
            prev["s1ev"] = ev
            wr = k.dma("gpsimd", s1_out[RT(tau, 128)],
                       s1_t[:, (tt % 2) * D:(tt % 2 + 1) * D], deps=[ev])
            prev["s1wr"][tt % 2] = wr
            last_s1_wr = wr
    mb.close()
    pm.close()
    wsc.close()
    k.emit("vector", lambda e: e.engine_nop(), deps=[last_s1_wr, xn_writes[-1]])
    pw.close()
    P.close()


def build_stage_c():
    k = KB("stage_c")
    io = {}
    io['x1_tm'] = k.dram("x1_tm", [T, D], F32, "ExternalInput")
    io['A_log'] = k.dram("A_log", [D, NST], F32, "ExternalInput")
    io['Dp'] = k.dram("Dp", [D], F32, "ExternalInput")
    io['W_xproj'] = k.dram("W_xproj", [D, D + 2 * NST], F32, "ExternalInput")
    io['b_xproj'] = k.dram("b_xproj", [D + 2 * NST], F32, "ExternalInput")
    io['W_out'] = k.dram("W_out", [D, D], F32, "ExternalInput")
    io['b_out'] = k.dram("b_out", [D], F32, "ExternalInput")
    io['scrB'] = k.dram("scrB", [NST, T], F32, "Internal")
    io['scrC'] = k.dram("scrC", [T, NST], BF16, "Internal")
    io['xn1_tm'] = k.dram("xn1_tm", [T, D], BF16, "ExternalOutput")
    io['s1_tm'] = k.dram("s1_tm", [T, D], F32, "ExternalOutput")
    with k.es:
        k.kcol(1e-5)
        k.kcol(1.0)
        emit_stage_c(k, io)
        k.finalize()
    return k


def emit_stage_d(k, io, debug=False):
    NHR = 4
    FF1 = 16384
    x1_hbm = io['x1_tm']; xn_hbm = io['xn1_tm']; s1_hbm = io['s1_tm']
    mk_h = io['rail_maker']; up_h = io['rail_up_proj']; wcp_h = io['w_cproj']
    scr_mk = io['scr_mk']; scr_up = io['scr_up']; scr_M = io['scr_M']
    scr_cp = io['scr_cp']; out_h = io['out']
    P = contextlib.ExitStack()
    xn_fm = k.sb(P, "xn_fm", [128, NCH * T], BF16)
    BDM = k.sb(P, "BDM", [128, 128 * 128], BF16)
    xn_loads = []
    for c in range(NCH):
        xn_loads.append(k.dma("sync", xn_fm[:, c * T:(c + 1) * T],
                              xn_hbm[:, RT(c, 128)], transpose=True))

    rp = contextlib.ExitStack()
    mk_f = k.sb(rp, "mk_f", [128, 512], F32)
    up_f = k.sb(rp, "up_f", [128, 2048], F32)
    mk_b = k.sb(rp, "mk_b", [128, 512], BF16)
    up_b = k.sb(rp, "up_b", [128, 2048], BF16)
    mkT = k.sb(rp, "mkT", [16, 32 * 128], BF16)
    upT = k.sb(rp, "upT", [16, 32 * 512], BF16)
    Mst = k.sb(rp, "Mst", [16, 256 * 64], BF16)
    prp = contextlib.ExitStack()
    ps_m = k.ps(prp, "ps_m", [16, 64])

    identb = k.sb(rp, "identb", [128, 128], BF16)
    mi = contextlib.ExitStack()
    iot_f = k.sb(mi, "iot_f", [128, 128], F32)
    iot_p = k.sb(mi, "iot_p", [128, 1], F32)
    c_if = k.emit("gpsimd", lambda e: e.iota(
        iot_f[:], [[1, 128]], channel_multiplier=0,
        allow_small_or_imprecise_dtypes=True))
    c_ip = k.emit("gpsimd", lambda e: e.iota(
        iot_p[:], [[1, 1]], channel_multiplier=1,
        allow_small_or_imprecise_dtypes=True))
    c_idb = k.emit("vector", lambda e: e.tensor_scalar(
        identb[:], iot_f[:], iot_p[:], None, ALU.is_equal), deps=[c_if, c_ip])
    mi.close()
    ldm = k.dma("sync", mk_f[:].rearrange("p (t j) -> p t j", j=16),
                mk_h.rearrange("h r i j -> (h r i) j").rearrange(
                    "(t p) j -> p t j", p=128), deps=[c_idb])
    cm1 = k.emit("vector", lambda e: e.tensor_copy(mk_b[:], mk_f[:]),
                 deps=[ldm])
    ldu = k.dma("sync", up_f[:].rearrange("p (t j) -> p t j", j=16),
                up_h.rearrange("h r q j -> (h r q) j").rearrange(
                    "(t p) j -> p t j", p=128))
    cu1 = k.emit("vector", lambda e: e.tensor_copy(up_b[:], up_f[:]),
                 deps=[ldu])
    pst = contextlib.ExitStack()
    ps_t16 = k.ps(pst, "ps_t16", [16, 128], BF16)
    tmk = []
    tup = []
    prev_t = None
    for h in range(NHR):
        for c in range(NCH):
            g = h * NCH + c
            t0 = h * 8 + c
            tr = k.emit("tensor", lambda e, t0=t0: e.transpose(
                ps_t16[:], mk_b[:, t0 * 16:(t0 + 1) * 16], identb[:]),
                deps=[cm1, c_idb, prev_t])
            cpy = k.emit("scalar", lambda e, g=g: e.copy(
                mkT[:, g * 128:(g + 1) * 128], ps_t16[:]), deps=[tr])
            prev_t = cpy
            tmk.append(cpy)
            for s in range(4):
                t1 = h * 32 + c * 4 + s
                tr = k.emit("tensor", lambda e, t1=t1: e.transpose(
                    ps_t16[:], up_b[:, t1 * 16:(t1 + 1) * 16], identb[:]),
                    deps=[cu1, prev_t])
                cpy = k.emit("scalar", lambda e, g=g, s=s: e.copy(
                    upT[:, g * 512 + s * 128: g * 512 + (s + 1) * 128],
                    ps_t16[:]), deps=[tr])
                prev_t = cpy
                tup.append(cpy)
    pst.close()
    mev = []
    prev_ev = None
    for h in range(NHR):
        for c in range(NCH):
            g = h * NCH + c
            for e2 in range(8):
                gi = (h * NCH + c) * 8 + e2
                mm = k.emit("tensor", lambda e, g=g, e2=e2: e.matmul(
                    ps_m[:], mkT[:, g * 128 + e2 * 16: g * 128 + (e2 + 1) * 16],
                    upT[:, g * 512 + e2 * 64: g * 512 + (e2 + 1) * 64],
                    start=True, stop=True),
                    deps=[tmk[-1], tup[-1], prev_ev] if gi == 0 or prev_ev
                    else [tmk[-1], tup[-1]])
                ev = k.emit("scalar", lambda e, gi=gi: e.copy(
                    Mst[:, gi * 64:(gi + 1) * 64], ps_m[:]), deps=[mm])
                prev_ev = ev
                mev.append(ev)
    if debug:
        dd1 = k.dma("gpsimd", io['mkT_dbg'], mkT[:], deps=[tmk[-1]])
        dd2 = k.dma("gpsimd", io['upT_dbg'], upT[:], deps=[tup[-1]])
        dd3 = k.dma("gpsimd", io['mst_dbg'], Mst[:], deps=[mev[-1]])
        if debug == 2:
            k.emit("vector", lambda e: e.engine_nop(), deps=[dd1, dd2, dd3])
            rp.close()
            prp.close()
            P.close()
            return
    wrM = k.dma("gpsimd", scr_M.rearrange("g i k -> i g k"),
                Mst[:].rearrange("i (g k) -> i g k", k=64), deps=[mev[-1]])
    zb = k.emit("gpsimd", lambda e: e.memset(BDM[:], 0.0), deps=[])
    asm = []
    for c in range(NCH):
        for h in range(NHR):
            for q in range(4):
                tb = ((c * NHR + h) * 4 + q) * 128
                for e2 in range(8):
                    g = (h * NCH + c) * 8 + e2
                    asm.append(k.dma(
                        "gpsimd",
                        BDM[:].rearrange("p f -> p f")[
                            16 * e2:16 * (e2 + 1),
                            tb + 16 * e2: tb + 16 * (e2 + 1)],
                        scr_M[g, :, q * 16:(q + 1) * 16],
                        deps=[wrM, zb] if not asm else []))
    rp.close()
    prp.close()

    if debug:
        dbb = k.dma("gpsimd", io['bdm_dbg'], BDM[:], deps=[asm[-1]])
        if debug == 3:
            k.emit("vector", lambda e: e.engine_nop(), deps=[dbb])
            rp.close()
            prp.close()
            P.close()
            return
    cp = contextlib.ExitStack()
    cst = k.sb(cp, "cst", [128, 2 * 2048], F32)
    cst2 = k.sb(cp, "cst2", [128, 2048], BF16)
    prev_cast = [None, None]
    prev_wr = None
    cp_writes = []
    for dt in range(8):
        for W in range(8):
            ib = (dt * 8 + W) % 2
            H = W // 2
            ld = k.dma("sync", cst[:, ib * 2048:(ib + 1) * 2048],
                       wcp_h[RT(dt, 128), W * 2048:(W + 1) * 2048],
                       deps=[prev_cast[ib]] if prev_cast[ib] else [wrM])
            src = cst[:, ib * 2048:(ib + 1) * 2048].rearrange(
                "p (c e q l) -> p c q e l", c=4, e=8, q=4)
            dst = cst2[:].rearrange("p (c q e l) -> p c q e l", c=4, q=4, e=8)
            ca = k.emit("vector", lambda e, src=src, dst=dst: e.tensor_copy(
                dst, src), deps=[ld, prev_wr])
            prev_cast[ib] = ca
            chalf = W % 2
            dstd = scr_cp[RT(dt, 128)].rearrange(
                "p (cc hh q r) -> p cc hh q r", cc=8, hh=4, q=4)[
                :, 4 * chalf:4 * chalf + 4, H]
            wr = k.dma("gpsimd", dstd,
                       cst2[:].rearrange("p (c q r) -> p c q r", c=4, q=4),
                       deps=[ca])
            prev_wr = wr
            cp_writes.append(wr)
    cp.close()

    mn = contextlib.ExitStack()
    h_fc = k.sb(mn, "h_fc", [128, 2 * 256], BF16)
    wcT = k.sb(mn, "wcT", [128, 2 * 1024], BF16)
    g_t = k.sb(mn, "g_t", [128, D], F32)
    x1_t = k.sb(mn, "x1_t", [128, D], F32)
    s1_t = k.sb(mn, "s1_t", [128, D], F32)
    o_t = k.sb(mn, "o_t", [128, 2 * D], F32)
    pmn = contextlib.ExitStack()
    ps_z = k.ps(pmn, "ps_z", [128, 2 * 256])
    ps_g = k.ps(pmn, "ps_g", [128, 4 * 512])

    prev = {"zev": [None, None], "wld": [None, None], "cpmm": [None, None],
            "gev": None, "owr": [None, None], "comb": None, "gv": None}
    last_wr = None
    for grp in range(8):
        for fc in range(128):
            c, rem = divmod(fc, 16)
            H, q = divmod(rem, 4)
            iz = fc % 2
            zmm = k.emit("tensor", lambda e, c=c, fc=fc, grp=grp, iz=iz: e.matmul(
                ps_z[:, iz * 256:(iz + 1) * 256],
                BDM[:, fc * 128:(fc + 1) * 128],
                xn_fm[:, c * T + grp * 256: c * T + (grp + 1) * 256],
                start=True, stop=True),
                deps=([asm[-1], xn_loads[-1], prev["zev"][iz]]
                      if (grp == 0 and fc < 2) else [prev["zev"][iz]]))
            hev = k.emit("scalar", lambda e, iz=iz: e.activation(
                h_fc[:, iz * 256:(iz + 1) * 256], ps_z[:, iz * 256:(iz + 1) * 256],
                AF.Gelu), deps=[zmm, prev["cpmm"][iz]] +
                ([cp_writes[-1]] if (grp == 0 and fc < 2) else []))
            prev["zev"][iz] = hev
            wld = k.dma("sync", wcT[:, iz * 1024:(iz + 1) * 1024],
                        scr_cp[0:D, fc * 128:(fc + 1) * 128],
                        deps=[cp_writes[-1], prev["wld"][iz]]
                        if (grp == 0 and fc < 2) else [prev["wld"][iz]],
                        transpose=True)
            mm = None
            for tsub in range(2):
                for half in range(2):
                    mm = k.emit("tensor",
                                lambda e, fc=fc, iz=iz, tsub=tsub, half=half: e.matmul(
                                    ps_g[:, (tsub * 2 + half) * 512:
                                         (tsub * 2 + half + 1) * 512],
                                    h_fc[:, iz * 256 + tsub * 128:
                                         iz * 256 + tsub * 128 + 128],
                                    wcT[:, iz * 1024 + half * 512:
                                        iz * 1024 + (half + 1) * 512],
                                    start=(fc == 0), stop=(fc == 127)),
                                deps=[hev, wld] +
                                ([prev["gev"]] if (fc == 0 and tsub == 0
                                                   and half == 0) else []))
            prev["wld"][iz] = mm
            prev["cpmm"][iz] = mm
        if debug and grp == 0:
            k.dma("gpsimd", io['h_dbg'], h_fc[:], deps=[prev["zev"][0], prev["zev"][1]])
        for tsub in range(2):
            tau = grp * 2 + tsub
            ldx = k.dma("gpsimd", x1_t[:], x1_hbm[RT(tau, 128)],
                        deps=[prev["comb"]])
            lds = k.dma("gpsimd", s1_t[:], s1_hbm[RT(tau, 128)], deps=[])
            gv = k.emit("scalar", lambda e, tsub=tsub: e.activation(
                g_t[:, 0:512], ps_g[:, tsub * 1024: tsub * 1024 + 512],
                AF.Silu), deps=[mm, prev["gv"]])
            gv2 = k.emit("scalar", lambda e, tsub=tsub: e.activation(
                g_t[:, 512:1024], ps_g[:, tsub * 1024 + 512: (tsub + 1) * 1024],
                AF.Silu), deps=[gv])
            prev["gv"] = gv2
            if tsub == 1:
                prev["gev"] = gv2
            ot = o_t[:, (tau % 2) * D:(tau % 2 + 1) * D]
            sg = k.emit("vector", lambda e, ot=ot: e.tensor_tensor(
                ot, s1_t[:], g_t[:], ALU.mult),
                deps=[gv2, lds, prev["owr"][tau % 2]])
            x2v = k.emit("vector", lambda e, ot=ot: e.tensor_tensor(
                ot, ot, x1_t[:], ALU.add), deps=[sg, ldx])
            prev["comb"] = x2v
            wr = k.dma("gpsimd", out_h[RT(tau, 128)], ot, deps=[x2v])
            prev["owr"][tau % 2] = wr
            last_wr = wr
    mn.close()
    pmn.close()
    k.emit("vector", lambda e: e.engine_nop(), deps=[last_wr])
    P.close()


def build_stage_d(debug=False):
    k = KB("stage_d")
    NHR = 4
    FF1 = 16384
    io = {}
    io['x1_tm'] = k.dram("x1_tm", [T, D], F32, "ExternalInput")
    io['xn1_tm'] = k.dram("xn1_tm", [T, D], BF16, "ExternalInput")
    io['s1_tm'] = k.dram("s1_tm", [T, D], F32, "ExternalInput")
    io['rail_maker'] = k.dram("rail_maker", [NHR, 64, 16, 16], F32, "ExternalInput")
    io['rail_up_proj'] = k.dram("rail_up_proj", [NHR, 64, 64, 16], F32, "ExternalInput")
    io['w_cproj'] = k.dram("w_cproj", [D, FF1], F32, "ExternalInput")
    io['scr_mk'] = k.dram("scr_mk", [NHR * 64 * 16, 16], BF16, "Internal")
    io['scr_up'] = k.dram("scr_up", [NHR * 64 * 64, 16], BF16, "Internal")
    io['scr_M'] = k.dram("scr_M", [256, 16, 64], BF16, "Internal")
    io['scr_cp'] = k.dram("scr_cp", [D, FF1], BF16, "Internal")
    io['out'] = k.dram("out", [T, D], F32, "ExternalOutput")
    if debug:
        io['bdm_dbg'] = k.dram("bdm_dbg", [128, 128 * 128], BF16, "ExternalOutput")
        io['h_dbg'] = k.dram("h_dbg", [128, 2 * 256], BF16, "ExternalOutput")
        io['mkT_dbg'] = k.dram("mkT_dbg", [16, 32 * 128], BF16, "ExternalOutput")
        io['upT_dbg'] = k.dram("upT_dbg", [16, 32 * 512], BF16, "ExternalOutput")
        io['mst_dbg'] = k.dram("mst_dbg", [16, 256 * 64], BF16, "ExternalOutput")
    with k.es:
        emit_stage_d(k, io, debug=debug)
        k.finalize()
    return k



def build_fused(dbg=False):
    k = KB("fused")
    NHR = 4
    FF1 = 16384
    inter = "ExternalOutput" if dbg else "Internal"
    io = {}
    io['x'] = k.dram("x", [T, D], F32, "ExternalInput")
    io['Wkqv'] = k.dram("Wkqv", [NH, D, EH], F32, "ExternalInput")
    io['bkqv'] = k.dram("bkqv", [NH, EH], F32, "ExternalInput")
    io['Wp'] = k.dram("Wp", [D, D], F32, "ExternalInput")
    io['bp'] = k.dram("bp", [D], F32, "ExternalInput")
    io['w_up'] = k.dram("w_up", [DFF, D], F32, "ExternalInput")
    io['w_down'] = k.dram("w_down", [D, DFF], F32, "ExternalInput")
    io['A_log'] = k.dram("A_log", [D, NST], F32, "ExternalInput")
    io['Dp'] = k.dram("Dp", [D], F32, "ExternalInput")
    io['W_xproj'] = k.dram("W_xproj", [D, D + 2 * NST], F32, "ExternalInput")
    io['b_xproj'] = k.dram("b_xproj", [D + 2 * NST], F32, "ExternalInput")
    io['W_out'] = k.dram("W_out", [D, D], F32, "ExternalInput")
    io['b_out'] = k.dram("b_out", [D], F32, "ExternalInput")
    io['rail_maker'] = k.dram("rail_maker", [NHR, 64, 16, 16], F32, "ExternalInput")
    io['rail_up_proj'] = k.dram("rail_up_proj", [NHR, 64, 64, 16], F32, "ExternalInput")
    io['w_cproj'] = k.dram("w_cproj", [D, FF1], F32, "ExternalInput")
    io['xn_tm'] = k.dram("xn_tm", [T, D], BF16, inter)
    io['s0_tm'] = k.dram("s0_tm", [T, D], F32, inter)
    io['wqu'] = k.dram("wqu", [DFF, D], BF16, "Internal")
    io['wqd'] = k.dram("wqd", [D, DFF], BF16, "Internal")
    io['x1_tm'] = k.dram("x1_tm", [T, D], F32, inter)
    io['scrB'] = k.dram("scrB", [NST, T], F32, "Internal")
    io['scrC'] = k.dram("scrC", [T, NST], BF16, "Internal")
    io['xn1_tm'] = k.dram("xn1_tm", [T, D], BF16, inter)
    io['s1_tm'] = k.dram("s1_tm", [T, D], F32, inter)
    io['scr_mk'] = k.dram("scr_mk", [4 * 64 * 16, 16], BF16, "Internal")
    io['scr_up'] = k.dram("scr_up", [4 * 64 * 64, 16], BF16, "Internal")
    io['scr_M'] = k.dram("scr_M", [256, 16, 64], BF16, "Internal")
    io['scr_cp'] = k.dram("scr_cp", [D, FF1], BF16, "Internal")
    io['out'] = k.dram("out", [T, D], F32, "ExternalOutput")
    with k.es:
        k.kcol(1e-5)
        k.kcol(SHIFT)
        k.kcol(1.0)
        emit_stage_a(k, io)
        k.barrier()
        emit_stage_b(k, io)
        k.barrier()
        emit_stage_c(k, io)
        k.barrier()
        emit_stage_d(k, io)
        k.finalize()
    return k


def kernel(L0_attn_x, L0_attn_Wkqv, L0_attn_bkqv, L0_attn_Wp, L0_attn_bp,
           L0_ffn_w_up, L0_ffn_w_down, L1_attn_A_log, L1_attn_Dp,
           L1_attn_W_xproj, L1_attn_b_xproj, L1_attn_W_out, L1_attn_b_out,
           L1_ffn_rail_maker, L1_ffn_rail_up_proj, L1_ffn_w_cproj):
    C = lambda a: np.ascontiguousarray(a)
    ncores = 8
    kf = build_fused()
    base = {
        "Wkqv": C(L0_attn_Wkqv), "bkqv": C(L0_attn_bkqv),
        "Wp": C(L0_attn_Wp), "bp": C(L0_attn_bp),
        "w_up": C(L0_ffn_w_up), "w_down": C(L0_ffn_w_down),
        "A_log": C(L1_attn_A_log), "Dp": C(L1_attn_Dp),
        "W_xproj": C(L1_attn_W_xproj), "b_xproj": C(L1_attn_b_xproj),
        "W_out": C(L1_attn_W_out), "b_out": C(L1_attn_b_out),
        "rail_maker": C(L1_ffn_rail_maker),
        "rail_up_proj": C(L1_ffn_rail_up_proj),
        "w_cproj": C(L1_ffn_w_cproj),
    }
    in_maps = [dict(base, x=C(L0_attn_x[i])) for i in range(ncores)]
    res = kf.run(in_maps)
    return np.stack([res.results[i]["out"] for i in range(ncores)], axis=0)
